# version 09
# baseline (speedup 1.0000x reference)
import numpy as np
import concourse.bass as bass
import concourse.mybir as mybir
from concourse.bass_utils import run_bass_kernel_spmd
from contextlib import ExitStack

F32 = mybir.dt.float32
I32 = mybir.dt.int32
BF16 = mybir.dt.bfloat16
AX = mybir.AxisListType.X
OP = mybir.AluOpType
AF_ = mybir.ActivationFunctionType

P = 128
NTOK = 4096
NT = NTOK // P
NBT = NT // 2
D = 1024
DC = D // P
FF = 3072
FC = FF // P
TC = NTOK // 512
NG = max(1, NTOK // 1024)
GSZ = NTOK // NG
EH = D // 512
INV7 = float(np.float32(1.0) / np.float32(7.0))
INVQ = float(np.float32(1.0) / np.float32(32767.0))
INV1024 = None
INV65536 = float(np.float32(1.0 / 65536.0))
EPS12 = 1e-12
SHIFT = 32768.5
SHIFTB = np.float32(32768.5)
UNSHIFT = -32768.0


def _refresh():
    global NT, NBT, DC, FC, TC, NG, GSZ, EH, INV1024
    NT = NTOK // P
    NBT = NT // 2
    DC = D // P
    FC = FF // P
    TC = NTOK // 512
    NG = max(1, NTOK // 1024)
    GSZ = NTOK // NG
    EH = D // 512
    INV1024 = float(np.float32(1.0) / np.float32(D))


_refresh()
ENGINES = ("sync", "tensor", "vector", "scalar", "gpsimd")

DBG = False

S_M1, S_S1, S_INV1, S_MG, S_SG, S_IGEFF, S_MUP, S_SUP, S_IUPEFF, \
    S_MT, S_B65, S_SG2, S_ITEFF, S_MU, S_PRE, S_SO, S_IUEFF, \
    S_T0, S_T1, S_T2, S_BG, S_X1 = range(22)
B_INV1, B_IG, B_SG, B_IUP, B_IT, B_IU, B_SO, B_BG = range(8)


class Sem:
    __slots__ = ("h", "v")

    def __init__(self, h):
        self.h = h
        self.v = 0


def build_nc(n_cores=8):
    nc = bass.Bass("TRN2", target_bir_lowering=False, detect_race_conditions=False)

    x_d = nc.dram_tensor("x", [NTOK, D], F32, kind="ExternalInput").ap()
    wqT_d = nc.dram_tensor("wqT", [D, D], F32, kind="ExternalInput").ap()
    wkT_d = nc.dram_tensor("wkT", [D, D], F32, kind="ExternalInput").ap()
    wvT_d = nc.dram_tensor("wvT", [D, D], F32, kind="ExternalInput").ap()
    wup_d = nc.dram_tensor("wup", [FF, D], F32, kind="ExternalInput").ap()
    wg_d = nc.dram_tensor("wg", [FF, D], F32, kind="ExternalInput").ap()
    wdn_d = nc.dram_tensor("wdn", [D, FF], F32, kind="ExternalInput").ap()
    out_d = nc.dram_tensor("out", [NTOK, D], F32, kind="ExternalOutput").ap()

    wupq_d = nc.dram_tensor("wupq", [FF, D], BF16).ap()
    wgq_d = nc.dram_tensor("wgq", [FF, D], BF16).ap()
    wdnq_d = nc.dram_tensor("wdnq", [D, FF], BF16).ap()
    up_h = nc.dram_tensor("up_h", [FC, P, NTOK], BF16).ap()
    gate_h = nc.dram_tensor("gate_h", [FC, P, NTOK], BF16).ap()
    t_h = nc.dram_tensor("t_h", [FC, P, NTOK], BF16).ap()
    u_h = nc.dram_tensor("u_h", [FC, P, NTOK], BF16).ap()
    sc_h = nc.dram_tensor("sc_h", [1, 8], F32).ap()
    colstage_d = nc.dram_tensor("colstage", [1, P], F32).ap()
    cc_in = [nc.dram_tensor(f"cc{i}_in", [1, 2], F32) for i in range(4)]
    cc_out = [nc.dram_tensor(f"cc{i}_out", [1, 2], F32) for i in range(4)]

    dbg = {}
    if DBG:
        dbg["x1"] = nc.dram_tensor("dbg_x1", [NT, P, D], BF16, kind="ExternalOutput").ap()
        dbg["scal"] = nc.dram_tensor("dbg_scal", [1, 24], F32, kind="ExternalOutput").ap()
        dbg["gf"] = nc.dram_tensor("dbg_gf", [2, D], F32, kind="ExternalOutput").ap()
        dbg["q1T"] = nc.dram_tensor("dbg_q1T", [P, DC, NTOK], BF16, kind="ExternalOutput").ap()
        dbg["cols"] = nc.dram_tensor("dbg_cols", [6, P, NT], F32, kind="ExternalOutput").ap()
        dbg["Qall"] = nc.dram_tensor("dbg_Qall", [P, NT // 2, D], BF16, kind="ExternalOutput").ap()
        dbg["gate0"] = nc.dram_tensor("dbg_gate0", [P, NTOK], BF16, kind="ExternalOutput").ap()
        dbg["wgq0"] = nc.dram_tensor("dbg_wgq0", [P, D], BF16, kind="ExternalOutput").ap()
        dbg["up0"] = nc.dram_tensor("dbg_up0", [P, NTOK], BF16, kind="ExternalOutput").ap()
        dbg["wupq0"] = nc.dram_tensor("dbg_wupq0", [P, D], BF16, kind="ExternalOutput").ap()
        dbg["t0"] = nc.dram_tensor("dbg_t0", [P, NTOK], BF16, kind="ExternalOutput").ap()
        dbg["u0"] = nc.dram_tensor("dbg_u0", [P, NTOK], BF16, kind="ExternalOutput").ap()
        dbg["uall"] = nc.dram_tensor("dbg_uall", [FC, P, NTOK], BF16, kind="ExternalOutput").ap()
        dbg["tall"] = nc.dram_tensor("dbg_tall", [FC, P, NTOK], BF16, kind="ExternalOutput").ap()
        dbg["upall"] = nc.dram_tensor("dbg_upall", [FC, P, NTOK], BF16, kind="ExternalOutput").ap()
        dbg["gall"] = nc.dram_tensor("dbg_gall", [FC, P, NTOK], BF16, kind="ExternalOutput").ap()
        dbg["wdnq"] = nc.dram_tensor("dbg_wdnq", [D, FF], BF16, kind="ExternalOutput").ap()
        dbg["wupq"] = nc.dram_tensor("dbg_wupq_full", [FF, D], BF16, kind="ExternalOutput").ap()

    ctx = ExitStack()
    with ctx:
        sbt = lambda name, shape, dt: ctx.enter_context(nc.sbuf_tensor(name, shape, dt))

        ident_b = sbt("ident_b", [P, P], BF16)
        ident_f = sbt("ident_f", [P, P], F32)
        ones_row = sbt("ones_row", [1, P], F32)
        c2048 = sbt("c2048", [P, 1], BF16)
        x1b = sbt("x1b", [P, NT, D], BF16)
        ssqq = sbt("ssqq", [P, NT], F32)
        ssqk = sbt("ssqk", [P, NT], F32)
        nq_c = sbt("nq_c", [P, NT], F32)
        nk_c = sbt("nk_c", [P, NT], F32)
        invq_c = sbt("invq_c", [P, NT], F32)
        invk_c = sbt("invk_c", [P, NT], F32)
        rsum_c = sbt("rsum_c", [P, NT], F32)
        rsq_c = sbt("rsq_c", [P, NT], F32)
        x1max_c = sbt("x1max_c", [P, NT], F32)
        gmax_c = sbt("gmax_c", [P, FC * TC], F32)
        upmax_c = sbt("upmax_c", [P, FC * TC], F32)
        tmax_c = sbt("tmax_c", [P, FC * TC], F32)
        umax_c = sbt("umax_c", [P, FC * 4], F32)
        tiny = sbt("tiny", [P, 8], F32)
        scal = sbt("scal", [1, 24], F32)
        rowred = sbt("rowred", [1, P], F32)
        bcol = sbt("bcol", [P, 8], F32)

        from math import prod as _prod

        class Arena:
            def __init__(self, name, nelems, dt):
                self.t = sbt(name, [P, nelems], dt)
                self.n = nelems
                self.off = 0
                self.stack = []

            def alloc(self, shape):
                assert shape[0] == P or shape[0] == 1
                n = _prod(shape[1:])
                off = (self.off + 15) // 16 * 16
                self.off = off + n
                assert self.off <= self.n, (self.t.name, self.off, self.n)
                ap = self.t.ap()[0:shape[0], off:off + n]
                if len(shape) == 3:
                    ap = ap.rearrange("p (a b) -> p a b", a=shape[1], b=shape[2])
                elif len(shape) == 4:
                    ap = ap.rearrange("p (a b c) -> p a b c", a=shape[1],
                                      b=shape[2], c=shape[3])
                return ap

            def push(self):
                self.stack.append(self.off)

            def pop(self):
                self.off = self.stack.pop()

        GSZ = 256 if NTOK >= 1024 else NTOK
        NG = NTOK // GSZ
        ARF = Arena("arena_f32", 6496, F32)
        AI = Arena("arena_i32", 4160, I32)
        AB = Arena("arena_b16", 49152, BF16)
        FMX = max(D, FF)

        CW0 = min(512, D, FF)
        ARF.push(); AI.push(); AB.push()
        wnat = [AI.alloc([P, CW0]).bitcast(F32) for _ in range(4)]
        ybuf0 = AI.alloc([P, CW0]).bitcast(F32)
        cbuf0 = AI.alloc([P, CW0])
        wdqb = [AB.alloc([P, CW0]) for _ in range(4)]
        bmax0 = ARF.alloc([P, CW0 // 32])
        sblk0 = ARF.alloc([P, CW0 // 32])
        isblk0 = ARF.alloc([P, CW0 // 32])
        wqkvT = AB.alloc([P, 3, DC, D])
        xbuf = [ARF.alloc([P, D]) for _ in range(2)]
        xTb = AB.alloc([P, DC, P])
        Qall = AB.alloc([P, NBT, D])
        _ktile1 = AB.alloc([P, D])
        _kvst1 = AB.alloc([P, D])

        class _OnePar:
            def __init__(self, ap):
                self.ap = ap

            def __getitem__(self, key):
                return self.ap[key[0], key[2]]

        ktile = _OnePar(_ktile1)
        kvst = _OnePar(_kvst1)
        sqscr = ARF.alloc([P, D])
        gf_sb = AI.alloc([1, D]).bitcast(F32)
        attn_t = ARF.alloc([P, D])
        r1_t = ARF.alloc([P, D])
        xa = ARF.alloc([P, D])
        gfb_sb = AB.alloc([P, D])
        ARF.pop(); AI.pop(); AB.pop()

        ARF.push(); AI.push(); AB.push()
        q1T = AB.alloc([P, DC, NTOK])
        wT_i = AB.alloc([P, 2, DC, P])
        mm_sb = [AB.alloc([P, NTOK]) for _ in range(2)]
        c2t = [AI.alloc([P, D]) for _ in range(2)]
        g_in = [AB.alloc([P, 512]) for _ in range(4)]
        p3cg = [AI.alloc([P, 512]) for _ in range(2)]
        p3sig = [AB.alloc([P, 512]) for _ in range(2)]
        p3csq = [AI.alloc([P, 512]) for _ in range(2)]
        t_sb = [AB.alloc([P, 512]) for _ in range(6)]
        ARF.pop(); AI.pop(); AB.pop()

        QW = 1024
        ND4 = 6
        ARF.push(); AI.push(); AB.push()
        p4t = [AB.alloc([P, QW]) for _ in range(ND4)]
        p4u = [AB.alloc([P, QW]) for _ in range(ND4)]
        p4c1 = [AI.alloc([P, QW]) for _ in range(2)]
        p4c2 = [AI.alloc([P, QW]) for _ in range(2)]
        p4yu = ARF.alloc([P, QW])
        uout = [AB.alloc([P, QW]) for _ in range(ND4)]
        ARF.pop(); AI.pop(); AB.pop()

        ARF.push(); AI.push(); AB.push()
        u_gr = [AB.alloc([P, FC, GSZ]) for _ in range(2)]
        wdnT = AB.alloc([P, FC, D])
        p5c = [AI.alloc([P, GSZ]) for _ in range(2)]
        r2_t = ARF.alloc([P, D])
        sq5 = ARF.alloc([P, D])
        out_sb = [ARF.alloc([P, D]) for _ in range(2)]
        ARF.pop(); AI.pop(); AB.pop()

        ps1 = ExitStack()
        psA = ps1.enter_context(nc.psum_tensor("psA", [P, D], F32))
        psK = ps1.enter_context(nc.psum_tensor("psK", [P, D], F32))
        psV = ps1.enter_context(nc.psum_tensor("psV", [P, D], F32))
        psGF = ps1.enter_context(nc.psum_tensor("psGF", [1, D], F32))
        ps1.close()
        ps2 = ExitStack()
        psTR = [ps2.enter_context(nc.psum_tensor(f"psTR{i}", [P, DC, P], BF16)) for i in range(2)]
        psMM = [ps2.enter_context(nc.psum_tensor(f"psMM{i}", [P, 512], F32)) for i in range(2)]
        ps2.close()
        ps5 = ExitStack()
        psDN = [ps5.enter_context(nc.psum_tensor(f"psDN{i}", [P, D], F32)) for i in range(2)]
        ps5.close()

        sem_names = ["ph", "g0", "g_wld", "p0_ld", "p0_v", "p0_sp",
                     "s_xld", "s_x2", "s_xtT", "s_xtS", "s_qkv", "s_ev",
                     "s_kvs", "s_gf", "s_gfe", "s_att", "s_v1b", "s_s1b",
                     "s_cc", "s_bc", "s_sc", "s_red",
                     "s_tr", "s_trS", "s_trV", "s_wld", "s_mm", "s_mev",
                     "s_mrv", "s_sp2",
                     "p3_ld", "p3_v", "p3_s", "p3_g", "p3_sp",
                     "p4_ld", "p4_v", "p4_s", "p4_g", "p4_sp", "p4_sp2",
                     "p5_uld", "p5_wld", "p5_v", "p5_s", "p5_g", "p5_t",
                     "p5_ev", "p5_out", "s_dbg", "fen_v", "fen_s", "fen_g"]
        sems = {n: ctx.enter_context(nc.semaphore(n)) for n in sem_names}

        block = ctx.enter_context(nc.Block())

        def walk(me, E):
            S = {k: Sem(v) for k, v in sems.items()}
            nbar = [0]

            def em(eng_name, fn, sem=None, n=1):
                inst = fn(E) if eng_name == me else None
                if sem is not None and n:
                    S[sem].v += n
                    if inst is not None:
                        inst.then_inc(S[sem].h, n)
                return S[sem].v if sem is not None else 0

            def wt(eng_name, sem, val):
                if eng_name == me and val > 0:
                    E.wait_ge(S[sem].h, val)

            def mark(sem):
                return S[sem].v

            FSEM = {"vector": "fen_v", "scalar": "fen_s", "gpsimd": "fen_g"}

            def fem(eng_name, fn, *_ignore, **_k):
                fs = FSEM[eng_name]
                em(eng_name, fn, fs, 1)
                wt(eng_name, fs, S[fs].v)

            def barrier():
                nbar[0] += 1
                em(me, lambda e: e.drain(), "ph", 1)
                S["ph"].v = 5 * nbar[0]
                wt(me, "ph", 5 * nbar[0])

            em("gpsimd", lambda e: e.memset(ident_b[:, :], 0.0), "g0", 1)
            wt("gpsimd", "g0", mark("g0"))
            em("gpsimd", lambda e: e.affine_select(
                out=ident_b[:, :], in_=ident_b[:, :], compare_op=OP.not_equal,
                fill=1.0, base=0, pattern=[[-1, P]], channel_multiplier=1),
               "g0", 1)
            em("gpsimd", lambda e: e.memset(ident_f[:, :], 0.0), "g0", 1)
            wt("gpsimd", "g0", mark("g0"))
            em("gpsimd", lambda e: e.affine_select(
                out=ident_f[:, :], in_=ident_f[:, :], compare_op=OP.not_equal,
                fill=1.0, base=0, pattern=[[-1, P]], channel_multiplier=1),
               "g0", 1)
            em("gpsimd", lambda e: e.memset(ones_row[:, :], 1.0), "g0", 1)
            em("gpsimd", lambda e: e.memset(scal[:, :], 0.0), "g0", 1)
            em("gpsimd", lambda e: e.memset(bcol[:, :], 0.0), "g0", 1)
            em("gpsimd", lambda e: e.memset(c2048[:, :], 1.0 / 2048.0), "g0", 1)
            g0_done = mark("g0")

            for wi, wsrc in enumerate([wqT_d, wkT_d, wvT_d]):
                em("gpsimd", lambda e, wi=wi, wsrc=wsrc: e.dma_start(
                    out=wqkvT[:, wi, :, :],
                    in_=wsrc.rearrange("(c p) e -> p c e", p=P)), "g_wld", 16)
            m_wqkv = mark("g_wld")

            NB0 = CW0 // 32
            p0_chunks = []
            for (w_in, wq_out, ntl, Fw) in [(wg_d, wgq_d, FC, D),
                                            (wup_d, wupq_d, FC, D),
                                            (wdn_d, wdnq_d, DC, FF)]:
                for j0 in range(ntl):
                    for cc in range(Fw // CW0):
                        p0_chunks.append((w_in, wq_out, j0, cc))
            NC0 = len(p0_chunks)
            p0_y = []
            p0_w = []
            p0_spm = []

            def p0_chunk(c):
                w_in, wq_out, j0, cc = p0_chunks[c]
                b = c % 4
                sp4 = c % 4
                csl = slice(cc * CW0, (cc + 1) * CW0)
                if c >= 4:
                    wt("gpsimd", "fen_v", p0_y[c - 4])
                em("gpsimd", lambda e, w_in=w_in, j0=j0, b=b, csl=csl: e.dma_start(
                    out=wnat[b][:, :], in_=w_in[j0 * P:(j0 + 1) * P, csl]),
                   "p0_ld", 16)
                if c == NC0 - 1:
                    for kd in range(2):
                        wt("gpsimd", "fen_v", p0_y[NC0 - 4 + kd])
                        em("gpsimd", lambda e, bd=(c + 1 + kd) % 4: e.dma_start(
                            out=wnat[bd][:, :], in_=wg_d[0:P, 0:CW0]),
                           "p0_ld", 16)
                wt("vector", "p0_ld", 16 * min(c + 3, NC0 + 2))
                fem("vector", lambda e, b=b: e.tensor_reduce(
                    bmax0[:, :], wnat[b][:, :].rearrange("p (nb k) -> p nb k", k=32),
                    AX, OP.max, apply_absolute_value=True))
                fem("vector", lambda e: e.tensor_scalar(
                    sblk0[:, :], bmax0[:, :], INV7, EPS12, OP.mult, OP.max))
                fem("vector", lambda e: e.reciprocal(isblk0[:, :], sblk0[:, :]))
                fem("vector", lambda e, b=b: e.tensor_tensor(
                    ybuf0[:, :].rearrange("p (nb k) -> p nb k", k=32),
                    wnat[b][:, :].rearrange("p (nb k) -> p nb k", k=32),
                    isblk0[:, :].unsqueeze(2).broadcast_to([P, NB0, 32]),
                    OP.mult))
                p0_y.append(mark("fen_v"))
                fem("vector", lambda e: e.tensor_copy(cbuf0[:, :], ybuf0[:, :]))
                if c >= 4:
                    wt("vector", "p0_sp", p0_spm[c - 4])
                em("vector", lambda e, sp4=sp4: e.tensor_tensor(
                    wdqb[sp4][:, :].rearrange("p (nb k) -> p nb k", k=32),
                    cbuf0[:, :].rearrange("p (nb k) -> p nb k", k=32),
                    sblk0[:, :].unsqueeze(2).broadcast_to([P, NB0, 32]),
                    OP.mult), "p0_v", 1)
                p0_w.append(mark("p0_v"))
                if c >= 2:
                    p0_spill(c - 2)

            def p0_spill(c):
                w_in, wq_out, j0, cc = p0_chunks[c]
                csl = slice(cc * CW0, (cc + 1) * CW0)
                wt("scalar", "p0_v", p0_w[c])
                em("scalar", lambda e, wq_out=wq_out, j0=j0, sp4=c % 4,
                       csl=csl: e.dma_start(
                    out=wq_out[j0 * P:(j0 + 1) * P, csl], in_=wdqb[sp4][:, :]),
                   "p0_sp", 16)
                p0_spm.append(mark("p0_sp"))

            p0_it = [0]

            def p0_drip():
                p0_it[0] += 1
                tgt = NC0 * p0_it[0] // (3 * NBT)
                while len(p0_w) < min(tgt, NC0):
                    p0_chunk(len(p0_w))

            wt("tensor", "g_wld", m_wqkv)
            wt("tensor", "g0", g0_done)
            xld_mk = []
            xtS_mk = []
            ev_mk = []
            kvs_mk = []
            gf_mk = []
            x2_mk = []
            v1b_r1 = []
            s1b_sq = []
            v1b_x1 = []
            att_mk = [None, None]
            gfe_ev = [None, None]

            def p1a_tile(j, astr):
                bat, jb = j // NBT, j % NBT
                par = j % 2
                xb = xa if astr else xbuf[par]
                if astr:
                    if jb >= 1:
                        wt("sync", "s_xtS", xtS_mk[j - 1])
                elif j >= 2:
                    wt("sync", "s_xtS", xtS_mk[j - 2])
                em("sync", lambda e, j=j, xb=xb: e.dma_start(
                    out=xb[:, :], in_=x_d[j * P:(j + 1) * P, :]), "s_xld", 16)
                xld_mk.append(mark("s_xld"))
                wt("tensor", "s_xld", xld_mk[j])
                if j >= 1:
                    wt("tensor", "s_ev", ev_mk[j - 1])
                if bat == 1 and jb == 0:
                    wt("tensor", "s_gfe", gfe_ev[0])
                for d in range(DC):
                    em("tensor", lambda e, xb=xb, d=d: e.transpose(
                        psA[:, :].rearrange("p (c q) -> p c q", q=P)[:, d, :],
                        xb[:, d * P:(d + 1) * P], ident_f[:, :]),
                       "s_xtT", 1 if d == DC - 1 else 0)
                tT = mark("s_xtT")
                wt("scalar", "s_xtT", tT)
                em("scalar", lambda e: e.activation(
                    xTb[:, :, :], psA[:, :].rearrange("p (c q) -> p c q", q=P),
                    AF_.Copy), "s_xtS", 1)
                xtS_mk.append(mark("s_xtS"))
                wt("tensor", "s_xtS", xtS_mk[j])
                if j >= 1:
                    wt("tensor", "s_kvs", kvs_mk[j - 1])
                for d in range(DC):
                    for half in range(EH):
                        sl = slice(half * 512, (half + 1) * 512)
                        em("tensor", lambda e, d=d, sl=sl: e.matmul(
                            psA[:, sl], xTb[:, d, :], wqkvT[:, 0, d, sl],
                            start=(d == 0), stop=(d == DC - 1)))
                        em("tensor", lambda e, d=d, sl=sl: e.matmul(
                            psK[:, sl], xTb[:, d, :], wqkvT[:, 1, d, sl],
                            start=(d == 0), stop=(d == DC - 1)))
                        em("tensor", lambda e, d=d, sl=sl: e.matmul(
                            psV[:, sl], xTb[:, d, :], wqkvT[:, 2, d, sl],
                            start=(d == 0), stop=(d == DC - 1)),
                           "s_qkv", 1 if (d == DC - 1 and half == EH - 1) else 0)
                qkv_mk = mark("s_qkv")
                wt("scalar", "s_qkv", qkv_mk)
                if j >= 1:
                    wt("scalar", "s_kvs", kvs_mk[j - 1])
                if bat == 1:
                    wt("scalar", "s_v1b", v1b_r1[jb])
                em("scalar", lambda e, j=j: e.activation(
                    sqscr[:, :], psA[:, :], AF_.Square,
                    accum_out=ssqq[:, j:j + 1]))
                em("scalar", lambda e, jb=jb: e.activation(
                    Qall[:, jb, :], psA[:, :], AF_.Copy))
                em("scalar", lambda e, j=j: e.activation(
                    sqscr[:, :], psK[:, :], AF_.Square,
                    accum_out=ssqk[:, j:j + 1]))
                em("scalar", lambda e, par=par: e.activation(
                    ktile[:, par, :], psK[:, :], AF_.Copy), "s_ev", 1)
                wt("scalar", "s_ev", mark("s_ev"))
                em("scalar", lambda e, j=j: e.activation(
                    nq_c[:, j:j + 1], ssqq[:, j:j + 1], AF_.Sqrt))
                em("scalar", lambda e, j=j: e.activation(
                    nk_c[:, j:j + 1], ssqk[:, j:j + 1], AF_.Sqrt), "s_ev", 1)
                ev_mk.append(mark("s_ev"))
                wt("vector", "s_ev", ev_mk[j])
                fem("vector", lambda e, j=j: e.tensor_scalar(
                    tiny[:, 0:1], nq_c[:, j:j + 1], EPS12, None, OP.max))
                fem("vector", lambda e, j=j: e.reciprocal(
                    invq_c[:, j:j + 1], tiny[:, 0:1]))
                fem("vector", lambda e, j=j: e.tensor_scalar(
                    tiny[:, 1:2], nk_c[:, j:j + 1], EPS12, None, OP.max))
                fem("vector", lambda e, j=j: e.reciprocal(
                    invk_c[:, j:j + 1], tiny[:, 1:2]))
                if j >= 1:
                    wt("vector", "s_gf", gf_mk[j - 1])
                em("vector", lambda e, j=j, par=par: e.scalar_tensor_tensor(
                    kvst[:, par, :], ktile[:, par, :], invk_c[:, j:j + 1],
                    psV[:, :], OP.mult, OP.mult), "s_kvs", 1)
                kvs_mk.append(mark("s_kvs"))
                wt("tensor", "s_kvs", kvs_mk[j])
                if bat == 1 and jb == 0:
                    wt("tensor", "s_gfe", gfe_ev[0])
                for half in range(EH):
                    sl = slice(half * 512, (half + 1) * 512)
                    em("tensor", lambda e, par=par, sl=sl, jb=jb: e.matmul(
                        psGF[:, sl], c2048[:, :], kvst[:, par, sl],
                        start=(jb == 0), stop=(jb == NBT - 1)),
                       "s_gf", 1 if half == EH - 1 else 0)
                gf_mk.append(mark("s_gf"))
                p0_drip()

            def gf_block(bat):
                wt("scalar", "s_gf", gf_mk[-1])
                if bat == 1:
                    wt("scalar", "s_att", att_mk[0])
                    wt("scalar", "s_v1b", v1b_r1[NBT - 1])
                em("scalar", lambda e: e.activation(
                    gf_sb[:, :], psGF[:, :], AF_.Copy), "s_gfe", 1)
                gfe_mk = mark("s_gfe")
                if DBG:
                    wt("sync", "s_gfe", gfe_mk)
                    em("sync", lambda e, bat=bat: e.dma_start(
                        out=dbg["gf"][bat:bat + 1, :], in_=gf_sb[:, :]),
                       "s_dbg", 16)
                wt("tensor", "s_gfe", gfe_mk)
                wt("tensor", "s_ev", ev_mk[-1])
                for half in range(EH):
                    sl = slice(half * 512, (half + 1) * 512)
                    em("tensor", lambda e, sl=sl: e.matmul(
                        psA[:, sl], ones_row[:, :], gf_sb[:, sl],
                        start=True, stop=True), "s_att",
                       1 if half == EH - 1 else 0)
                att_mk[bat] = mark("s_att")
                wt("scalar", "s_att", att_mk[bat])
                em("scalar", lambda e: e.activation(
                    gfb_sb[:, :], psA[:, :], AF_.Copy), "s_gfe", 1)
                gfe_ev[bat] = mark("s_gfe")

            def p1b_tile(j):
                bat, jb = j // NBT, j % NBT
                par = j % 2
                if jb >= 2:
                    wt("sync", "s_v1b", v1b_r1[j - 2])
                elif bat == 0:
                    wt("sync", "s_xtS", xtS_mk[NBT - 2 + jb])
                else:
                    wt("sync", "s_v1b", v1b_r1[NBT - 2 + jb])
                em("sync", lambda e, j=j, par=par: e.dma_start(
                    out=xbuf[par][:, :], in_=x_d[j * P:(j + 1) * P, :]),
                   "s_x2", 16)
                x2_mk.append(mark("s_x2"))
                wt("vector", "s_gfe", gfe_ev[bat])
                wt("vector", "s_x2", x2_mk[-1])
                if jb >= 1:
                    wt("vector", "s_s1b", s1b_sq[j - 1])
                elif bat == 1:
                    wt("vector", "s_s1b", s1b_sq[-1])
                fem("vector", lambda e, jb=jb, j=j: e.scalar_tensor_tensor(
                    attn_t[:, :], Qall[:, jb, :], invq_c[:, j:j + 1],
                    gfb_sb[:, :], OP.mult, OP.mult))
                em("vector", lambda e, j=j, par=par: e.scalar_tensor_tensor(
                    r1_t[:, :], attn_t[:, :], 0.0, xbuf[par][:, :],
                    OP.add, OP.add, accum_out=rsum_c[:, j:j + 1]), "s_v1b", 1)
                v1b_r1.append(mark("s_v1b"))
                wt("scalar", "s_v1b", v1b_r1[-1])
                em("scalar", lambda e, j=j: e.activation(
                    sqscr[:, :], r1_t[:, :], AF_.Square,
                    accum_out=rsq_c[:, j:j + 1]), "s_s1b", 1)
                s1b_sq.append(mark("s_s1b"))
                wt("vector", "s_s1b", s1b_sq[-1])
                fem("vector", lambda e, j=j: e.tensor_scalar(
                    tiny[:, 2:3], rsum_c[:, j:j + 1], INV1024, None, OP.mult))
                fem("vector", lambda e: e.tensor_tensor(
                    tiny[:, 3:4], tiny[:, 2:3], tiny[:, 2:3], OP.mult))
                em("vector", lambda e, j=j: e.scalar_tensor_tensor(
                    tiny[:, 4:5], rsq_c[:, j:j + 1], INV1024, tiny[:, 3:4],
                    OP.mult, OP.subtract), "s_v1b", 1)
                vv = mark("s_v1b")
                wt("scalar", "s_v1b", vv)
                em("scalar", lambda e: e.activation(
                    tiny[:, 5:6], tiny[:, 4:5], AF_.Sqrt, bias=np.float32(1e-5)),
                   "s_s1b", 1)
                sv = mark("s_s1b")
                wt("vector", "s_s1b", sv)
                fem("vector", lambda e: e.reciprocal(tiny[:, 6:7], tiny[:, 5:6]))
                fem("vector", lambda e, j=j: e.tensor_scalar(
                    x1b[:, j, :], r1_t[:, :], tiny[:, 2:3], tiny[:, 6:7],
                    OP.subtract, OP.mult))
                em("vector", lambda e, j=j: e.tensor_reduce(
                    x1max_c[:, j:j + 1], x1b[:, j, :], AX, OP.max,
                    apply_absolute_value=True), "s_v1b", 1)
                p0_drip()

            for jb in range(NBT):
                p1a_tile(jb, False)
            gf_block(0)
            for jb in range(NBT):
                p1b_tile(jb)
                p1a_tile(NBT + jb, True)
            gf_block(1)
            for jb in range(NBT):
                p1b_tile(NBT + jb)
            while len(p0_w) < NC0:
                p0_chunk(len(p0_w))
            p0_spill(NC0 - 2)
            p0_spill(NC0 - 1)
            wt("scalar", "p0_sp", p0_spm[-1])
            barrier()

            if DBG:
                em("sync", lambda e: e.dma_start(
                    out=dbg["x1"].rearrange("n p d -> p n d"),
                    in_=x1b[:, :, :]), "s_dbg", 16)
                em("sync", lambda e: e.dma_start(
                    out=dbg["Qall"], in_=Qall[:, :, :]), "s_dbg", 16)
                for q, colt in enumerate([invq_c, invk_c, rsum_c, rsq_c,
                                          ssqq, ssqk]):
                    em("sync", lambda e, q=q, colt=colt: e.dma_start(
                        out=dbg["cols"].rearrange("k p n -> p k n")[:, q, :],
                        in_=colt[:, :]), "s_dbg", 16)

            def global_max_to(cols_list, cc_idx, scal_dst_list):
                nred = len(cols_list)
                for q, (cols, ncols) in enumerate(cols_list):
                    em("vector", lambda e, cols=cols, ncols=ncols: e.tensor_reduce(
                        tiny[:, 0:1], cols[:, 0:ncols], AX, OP.max), "s_red", 1)
                    wt("sync", "s_red", mark("s_red"))
                    em("sync", lambda e: e.dma_start(
                        out=colstage_d.rearrange("o p -> p o"),
                        in_=tiny[:, 0:1]), "s_bc", 16)
                    wt("sync", "s_bc", mark("s_bc"))
                    em("sync", lambda e: e.dma_start(
                        out=rowred[:, :], in_=colstage_d), "s_bc", 16)
                    wt("vector", "s_bc", mark("s_bc"))
                    em("vector", lambda e, q=q: e.tensor_reduce(
                        scal[:, S_T0 + q:S_T0 + q + 1], rowred[:, :], AX,
                        OP.max), "s_red", 1)
                wt("sync", "s_red", mark("s_red"))
                em("sync", lambda e, cc_idx=cc_idx, nred=nred: e.dma_start(
                    out=cc_in[cc_idx].ap()[0:1, 0:nred],
                    in_=scal[:, S_T0:S_T0 + nred]), "s_bc", 16)
                wt("gpsimd", "s_bc", mark("s_bc"))
                em("gpsimd", lambda e, cc_idx=cc_idx, nred=nred: e.collective_compute(
                    "AllReduce", OP.max, replica_groups=[list(range(n_cores))],
                    ins=[cc_in[cc_idx].ap()[0:1, 0:nred]],
                    outs=[cc_out[cc_idx].ap()[0:1, 0:nred]]), "s_cc", 1)
                wt("sync", "s_cc", mark("s_cc"))
                for q, dstn in enumerate(scal_dst_list):
                    em("sync", lambda e, cc_idx=cc_idx, q=q, dstn=dstn: e.dma_start(
                        out=scal[:, dstn:dstn + 1],
                        in_=cc_out[cc_idx].ap()[0:1, q:q + 1]), "s_bc", 16)
                wt("vector", "s_bc", mark("s_bc"))
                wt("sync", "s_bc", mark("s_bc"))

            def bcast(scal_idx, bcol_idx, hb_idx):
                wt("sync", "s_sc", mark("s_sc"))
                em("sync", lambda e, scal_idx=scal_idx, hb_idx=hb_idx: e.dma_start(
                    out=sc_h[0:1, hb_idx:hb_idx + 1],
                    in_=scal[:, scal_idx:scal_idx + 1]), "s_bc", 16)
                wt("sync", "s_bc", mark("s_bc"))
                em("sync", lambda e, bcol_idx=bcol_idx, hb_idx=hb_idx: e.dma_start(
                    out=bcol[:, bcol_idx:bcol_idx + 1],
                    in_=sc_h[0:1, hb_idx:hb_idx + 1].broadcast_to([P, 1])),
                   "s_bc", 16)
                wt("sync", "s_bc", mark("s_bc"))

            global_max_to([(x1max_c, NT)], 0, [S_M1])
            fem("vector", lambda e: e.tensor_scalar(
                scal[:, S_S1:S_S1 + 1], scal[:, S_M1:S_M1 + 1], INVQ, EPS12,
                OP.mult, OP.max))
            fem("vector", lambda e: e.reciprocal(
                scal[:, S_INV1:S_INV1 + 1], scal[:, S_S1:S_S1 + 1]))
            em("vector", lambda e: e.engine_nop(), "s_sc", 1)
            bcast(S_INV1, B_INV1, 0)
            barrier()

            trS_mk = []
            trV_mk = []
            for j in range(NT):
                par = j % 2
                if j >= 2:
                    wt("tensor", "s_trS", trS_mk[j - 2])
                for d in range(DC):
                    em("tensor", lambda e, j=j, d=d, par=par: e.transpose(
                        psTR[par][:, d, :], x1b[:, j, d * P:(d + 1) * P],
                        ident_b[:, :]), "s_tr", 1 if d == DC - 1 else 0)
                trm = mark("s_tr")
                wt("scalar", "s_tr", trm)
                if j >= 2:
                    wt("scalar", "s_trV", trV_mk[j - 2][1])
                em("scalar", lambda e, par=par: e.activation(
                    c2t[par][:, :],
                    psTR[par][:, :, :].rearrange("p c q -> p (c q)"),
                    AF_.Identity, scale=bcol[:, B_INV1:B_INV1 + 1]),
                   "s_trS", 1)
                trS_mk.append(mark("s_trS"))
                wt("vector", "s_trS", trS_mk[j])
                em("vector", lambda e, j=j, par=par: e.tensor_scalar(
                    q1T[:, :, j * P:(j + 1) * P],
                    c2t[par][:, :].rearrange("p (c q) -> p c q", q=P),
                    1.0, None, OP.mult), "s_trV", 1)
                trV_mk.append((mark("s_trV"), mark("s_trV")))
            if DBG:
                wt("sync", "s_trV", trV_mk[-1][1])
                em("sync", lambda e: e.dma_start(out=dbg["q1T"], in_=q1T[:, :, :]),
                   "s_dbg", 16)

            def mm_phase(wq_src, out_hbm, max_cols, p3=False):
                mm_grp = []
                wld_i = []
                mev_mks = []
                mrv_mks = []
                sp_mks = []
                p3_cg, p3_csq, p3_tt, p3_spm, p3_ldm = [], [], [], [], []

                def load_wi(i):
                    wpar = i % 2
                    for d in range(DC):
                        em("sync", lambda e, i=i, d=d, wpar=wpar,
                               wq_src=wq_src: e.dma_start_transpose(
                            wT_i[:, wpar, d, :],
                            wq_src[i * P:(i + 1) * P, d * P:(d + 1) * P]),
                           "s_wld", 16)
                    wld_i.append(mark("s_wld"))

                NK3 = FC * TC

                def p3_spill(k):
                    i3, tc3 = k // TC, k % TC
                    csl = slice(tc3 * 512, (tc3 + 1) * 512)
                    wt("gpsimd", "p3_v", p3_tt[k])
                    em("gpsimd", lambda e, i3=i3, par=k % 6, csl=csl: e.dma_start(
                        out=t_h[i3, :, csl], in_=t_sb[par][:, :]), "p3_sp", 16)
                    p3_spm.append(mark("p3_sp"))

                def p3_sub(k):
                    i3, tc3 = k // TC, k % TC
                    par = k % 2
                    qar = k % 4
                    sar = k % 6
                    csl = slice(tc3 * 512, (tc3 + 1) * 512)
                    if k >= 4:
                        p3_spill(k - 4)
                    if k >= 4:
                        wt("gpsimd", "fen_s", p3_cg[k - 4])
                    em("gpsimd", lambda e, i3=i3, qar=qar, csl=csl: e.dma_start(
                        out=g_in[qar][:, :], in_=gate_h[i3, :, csl]), "p3_ld", 16)
                    p3_ldm.append(mark("p3_ld"))
                    if k == NK3 - 1:
                        for kd in range(2):
                            qd = (k + 1 + kd) % 4
                            wt("gpsimd", "fen_s", p3_cg[NK3 - 4 + kd])
                            em("gpsimd", lambda e, qd=qd: e.dma_start(
                                out=g_in[qd][:, :], in_=gate_h[0, :, 0:512]),
                               "p3_ld", 16)
                    wt("scalar", "p3_ld", 16 * min(k + 3, NK3 + 2))
                    if k >= 2:
                        wt("scalar", "p3_v", p3_tt[k - 2])
                    fem("scalar", lambda e, par=par, qar=qar: e.activation(
                        p3cg[par][:, :], g_in[qar][:, :], AF_.Identity,
                        scale=bcol[:, B_IG:B_IG + 1]))
                    p3_cg.append(mark("fen_s"))
                    fem("scalar", lambda e, par=par: e.activation(
                        p3sig[par][:, :], p3cg[par][:, :], AF_.Sigmoid,
                        scale=bcol[:, B_SG:B_SG + 1]))
                    em("scalar", lambda e, par=par: e.activation(
                        p3csq[par][:, :], p3sig[par][:, :], AF_.Identity,
                        scale=np.float32(65536.0)), "p3_s", 1)
                    p3_csq.append(mark("p3_s"))
                    wt("vector", "p3_s", p3_csq[-1])
                    if k >= 6:
                        wt("vector", "p3_sp", p3_spm[k - 6])
                    em("vector", lambda e, par=par, sar=sar: e.tensor_tensor(
                        t_sb[sar][:, :], p3cg[par][:, :], p3csq[par][:, :],
                        OP.mult), "p3_v", 1)
                    p3_tt.append(mark("p3_v"))
                    wt("vector", "p3_v", p3_tt[-1])
                    em("vector", lambda e, k=k, sar=sar: e.tensor_reduce(
                        tmax_c[:, k:k + 1], t_sb[sar][:, :], AX, OP.max,
                        apply_absolute_value=True), "p3_v", 1)

                load_wi(0)
                wt("tensor", "s_trV", trV_mk[-1][1])
                for i in range(FC):
                    spar = i % 2
                    wpar = i % 2
                    if i + 1 < FC:
                        if i >= 1:
                            wt("sync", "s_mm", mm_grp[(i - 1) * TC + TC - 1])
                        load_wi(i + 1)
                    wt("tensor", "s_wld", wld_i[i])
                    for tc in range(TC):
                        gidx = i * TC + tc
                        mpar = gidx % 2
                        if gidx >= 2:
                            wt("tensor", "s_mev", mev_mks[gidx - 2])
                        for d in range(DC):
                            em("tensor", lambda e, d=d, tc=tc, mpar=mpar,
                                   wpar=wpar: e.matmul(
                                psMM[mpar][:, :],
                                wT_i[:, wpar, d, :],
                                q1T[:, d, tc * 512:(tc + 1) * 512],
                                start=(d == 0), stop=(d == DC - 1)),
                               "s_mm", 1 if d == DC - 1 else 0)
                        mm_grp.append(mark("s_mm"))
                        wt("vector", "s_mm", mm_grp[-1])
                        if tc == 0 and i >= 2:
                            wt("vector", "s_sp2", sp_mks[i - 2])
                        em("vector", lambda e, spar=spar, tc=tc, mpar=mpar:
                           e.tensor_copy(
                            mm_sb[spar][:, tc * 512:(tc + 1) * 512],
                            psMM[mpar][:, :]), "s_mev", 1)
                        mev_mks.append(mark("s_mev"))
                        wt("vector", "s_mev", mev_mks[-1])
                        em("vector", lambda e, spar=spar, tc=tc, gidx=gidx,
                               max_cols=max_cols: e.tensor_reduce(
                            max_cols[:, gidx:gidx + 1],
                            mm_sb[spar][:, tc * 512:(tc + 1) * 512], AX, OP.max,
                            apply_absolute_value=True), "s_mrv", 1)
                        if p3:
                            p3_sub(gidx)
                    mrv_mks.append(mark("s_mrv"))
                    wt("sync", "s_mev", mev_mks[-1])
                    em("sync", lambda e, i=i, spar=spar, out_hbm=out_hbm: e.dma_start(
                        out=out_hbm[i, :, :], in_=mm_sb[spar][:, :]), "s_sp2", 16)
                    sp_mks.append(mark("s_sp2"))
                wt("sync", "s_sp2", sp_mks[-1])
                wt("vector", "s_mrv", mark("s_mrv"))
                if p3:
                    for kt in range(FC * TC - 4, FC * TC):
                        p3_spill(kt)
                    wt("gpsimd", "p3_sp", p3_spm[-1])

            mm_phase(wgq_d, gate_h, gmax_c)
            barrier()

            if DBG:
                em("sync", lambda e: e.dma_start(
                    out=dbg["gate0"], in_=gate_h[0, :, :]), "s_dbg", 16)
                em("sync", lambda e: e.dma_start(
                    out=dbg["wgq0"], in_=wgq_d[0:P, :]), "s_dbg", 16)
            global_max_to([(gmax_c, FC * TC)], 1, [S_MG])
            fem("vector", lambda e: e.tensor_tensor(
                scal[:, S_T2:S_T2 + 1], scal[:, S_S1:S_S1 + 1],
                scal[:, S_MG:S_MG + 1], OP.mult))
            fem("vector", lambda e: e.tensor_scalar(
                scal[:, S_SG:S_SG + 1], scal[:, S_T2:S_T2 + 1], INVQ, EPS12,
                OP.mult, OP.max))
            fem("vector", lambda e: e.reciprocal(
                scal[:, S_T2:S_T2 + 1], scal[:, S_SG:S_SG + 1]))
            fem("vector", lambda e: e.tensor_tensor(
                scal[:, S_IGEFF:S_IGEFF + 1], scal[:, S_S1:S_S1 + 1],
                scal[:, S_T2:S_T2 + 1], OP.mult))
            em("vector", lambda e: e.engine_nop(), "s_sc", 1)
            bcast(S_IGEFF, B_IG, 1)
            bcast(S_SG, B_SG, 2)
            barrier()

            mm_phase(wupq_d, up_h, upmax_c, p3=True)
            barrier()

            if DBG:
                em("sync", lambda e: e.dma_start(
                    out=dbg["up0"], in_=up_h[0, :, :]), "s_dbg", 16)
                em("sync", lambda e: e.dma_start(
                    out=dbg["wupq0"], in_=wupq_d[0:P, :]), "s_dbg", 16)
                em("sync", lambda e: e.dma_start(
                    out=dbg["wupq"], in_=wupq_d[:, :]), "s_dbg", 16)
                em("sync", lambda e: e.dma_start(
                    out=dbg["t0"], in_=t_h[0, :, :]), "s_dbg", 16)
            global_max_to([(tmax_c, FC * TC), (upmax_c, FC * TC)], 2, [S_MT, S_MUP])
            fem("vector", lambda e: e.tensor_tensor(
                scal[:, S_T2:S_T2 + 1], scal[:, S_S1:S_S1 + 1],
                scal[:, S_MUP:S_MUP + 1], OP.mult))
            fem("vector", lambda e: e.tensor_scalar(
                scal[:, S_SUP:S_SUP + 1], scal[:, S_T2:S_T2 + 1], INVQ, EPS12,
                OP.mult, OP.max))
            fem("vector", lambda e: e.reciprocal(
                scal[:, S_T2:S_T2 + 1], scal[:, S_SUP:S_SUP + 1]))
            fem("vector", lambda e: e.tensor_tensor(
                scal[:, S_IUPEFF:S_IUPEFF + 1], scal[:, S_S1:S_S1 + 1],
                scal[:, S_T2:S_T2 + 1], OP.mult))
            fem("vector", lambda e: e.tensor_scalar(
                scal[:, S_B65:S_B65 + 1], scal[:, S_SG:S_SG + 1], INV65536,
                None, OP.mult))
            fem("vector", lambda e: e.tensor_tensor(
                scal[:, S_T2:S_T2 + 1], scal[:, S_B65:S_B65 + 1],
                scal[:, S_MT:S_MT + 1], OP.mult))
            fem("vector", lambda e: e.tensor_scalar(
                scal[:, S_SG2:S_SG2 + 1], scal[:, S_T2:S_T2 + 1], INVQ, EPS12,
                OP.mult, OP.max))
            fem("vector", lambda e: e.reciprocal(
                scal[:, S_T2:S_T2 + 1], scal[:, S_SG2:S_SG2 + 1]))
            fem("vector", lambda e: e.tensor_tensor(
                scal[:, S_ITEFF:S_ITEFF + 1], scal[:, S_B65:S_B65 + 1],
                scal[:, S_T2:S_T2 + 1], OP.mult))
            em("vector", lambda e: e.engine_nop(), "s_sc", 1)
            bcast(S_IUPEFF, B_IUP, 3)
            bcast(S_ITEFF, B_IT, 4)
            barrier()

            NPQ = NTOK // QW
            NK4 = NPQ * FC
            p4tld, p4uld, p4c1m, p4yum, p4ttm, p4spm = [], [], [], [], [], []
            for k in range(NK4):
                i, h = k // NPQ, k % NPQ
                par = k % 2
                qar = k % ND4
                csl = slice(h * QW, (h + 1) * QW)
                if k >= ND4:
                    wt("sync", "fen_s", p4c1m[k - ND4])
                    wt("gpsimd", "fen_v", p4yum[k - ND4])
                em("sync", lambda e, i=i, qar=qar, csl=csl: e.dma_start(
                    out=p4t[qar][:, :], in_=t_h[i, :, csl]), "p4_ld", 16)
                p4tld.append(mark("p4_ld"))
                em("gpsimd", lambda e, i=i, qar=qar, csl=csl: e.dma_start(
                    out=p4u[qar][:, :], in_=up_h[i, :, csl]), "p4_g", 16)
                p4uld.append(mark("p4_g"))
                if k == NK4 - 1:
                    for kd in range(2):
                        qd = (k + 1 + kd) % ND4
                        wt("sync", "fen_s", p4c1m[NK4 - ND4 + kd])
                        em("sync", lambda e, qd=qd: e.dma_start(
                            out=p4t[qd][:, :], in_=t_h[0, :, 0:QW]),
                           "p4_ld", 16)
                        wt("gpsimd", "fen_v", p4yum[NK4 - ND4 + kd])
                        em("gpsimd", lambda e, qd=qd: e.dma_start(
                            out=p4u[qd][:, :], in_=up_h[0, :, 0:QW]),
                           "p4_g", 16)
                if k >= 2:
                    ks = k - 2
                    wt("scalar", "p4_v", p4ttm[ks])
                    em("scalar", lambda e, i2=ks // NPQ, qar2=ks % ND4,
                           csl2=slice((ks % NPQ) * QW, (ks % NPQ + 1) * QW):
                       e.dma_start(out=u_h[i2, :, csl2], in_=uout[qar2][:, :]),
                       "p4_sp", 16)
                    p4spm.append(("p4_sp", mark("p4_sp")))
                wt("scalar", "p4_ld", 16 * min(k + 3, NK4 + 2))
                if k >= 2:
                    wt("scalar", "p4_v", p4ttm[k - 2])
                fem("scalar", lambda e, par=par, qar=qar: e.activation(
                    p4c1[par][:, :], p4t[qar][:, :], AF_.Identity,
                    scale=bcol[:, B_IT:B_IT + 1]))
                p4c1m.append(mark("fen_s"))
                wt("vector", "p4_g", 16 * min(k + 3, NK4 + 2))
                fem("vector", lambda e, qar=qar: e.tensor_scalar(
                    p4yu[:, :], p4u[qar][:, :], bcol[:, B_IUP:B_IUP + 1],
                    None, OP.mult))
                p4yum.append(mark("fen_v"))
                fem("vector", lambda e, par=par: e.tensor_copy(
                    p4c2[par][:, :], p4yu[:, :]))
                wt("vector", "fen_s", p4c1m[-1])
                if k >= ND4:
                    wt("vector", p4spm[k - ND4][0], p4spm[k - ND4][1])
                em("vector", lambda e, par=par, qar=qar: e.tensor_tensor(
                    uout[qar][:, :], p4c1[par][:, :], p4c2[par][:, :],
                    OP.mult), "p4_v", 1)
                p4ttm.append(mark("p4_v"))
                wt("vector", "p4_v", p4ttm[-1])
                em("vector", lambda e, k=k, qar=qar: e.tensor_reduce(
                    umax_c[:, k:k + 1], uout[qar][:, :], AX, OP.max,
                    apply_absolute_value=True), "p4_v", 1)
            for ks in range(NK4 - 2, NK4):
                wt("scalar", "p4_v", p4ttm[ks])
                em("scalar", lambda e, i2=ks // NPQ, qar2=ks % ND4,
                       csl2=slice((ks % NPQ) * QW, (ks % NPQ + 1) * QW):
                   e.dma_start(out=u_h[i2, :, csl2], in_=uout[qar2][:, :]),
                   "p4_sp", 16)
                p4spm.append(("p4_sp", mark("p4_sp")))
            wt("scalar", "p4_sp", mark("p4_sp"))
            barrier()

            global_max_to([(umax_c, NPQ * FC)], 3, [S_MU])
            fem("vector", lambda e: e.tensor_tensor(
                scal[:, S_PRE:S_PRE + 1], scal[:, S_SG2:S_SG2 + 1],
                scal[:, S_SUP:S_SUP + 1], OP.mult))
            fem("vector", lambda e: e.tensor_tensor(
                scal[:, S_T2:S_T2 + 1], scal[:, S_PRE:S_PRE + 1],
                scal[:, S_MU:S_MU + 1], OP.mult))
            fem("vector", lambda e: e.tensor_scalar(
                scal[:, S_SO:S_SO + 1], scal[:, S_T2:S_T2 + 1], INVQ, EPS12,
                OP.mult, OP.max))
            fem("vector", lambda e: e.reciprocal(
                scal[:, S_T2:S_T2 + 1], scal[:, S_SO:S_SO + 1]))
            fem("vector", lambda e: e.tensor_tensor(
                scal[:, S_IUEFF:S_IUEFF + 1], scal[:, S_PRE:S_PRE + 1],
                scal[:, S_T2:S_T2 + 1], OP.mult))
            em("vector", lambda e: e.engine_nop(), "s_sc", 1)
            bcast(S_IUEFF, B_IU, 5)
            bcast(S_SO, B_SO, 6)
            if DBG:
                wt("sync", "s_bc", mark("s_bc"))
                em("sync", lambda e: e.dma_start(out=dbg["scal"], in_=scal[:, :]),
                   "s_dbg", 16)
                for i4 in range(FC):
                    em("sync", lambda e, i4=i4: e.dma_start(
                        out=dbg["uall"][i4, :, :], in_=u_h[i4, :, :]),
                       "s_dbg", 16)
                    em("sync", lambda e, i4=i4: e.dma_start(
                        out=dbg["tall"][i4, :, :], in_=t_h[i4, :, :]),
                       "s_dbg", 16)
                    em("sync", lambda e, i4=i4: e.dma_start(
                        out=dbg["upall"][i4, :, :], in_=up_h[i4, :, :]),
                       "s_dbg", 16)
                    em("sync", lambda e, i4=i4: e.dma_start(
                        out=dbg["gall"][i4, :, :], in_=gate_h[i4, :, :]),
                       "s_dbg", 16)
                em("sync", lambda e: e.dma_start(out=dbg["u0"], in_=u_h[0, :, :]),
                   "s_dbg", 16)
                em("sync", lambda e: e.dma_start(out=dbg["wdnq"], in_=wdnq_d),
                   "s_dbg", 16)
                wt("sync", "s_dbg", mark("s_dbg"))
            barrier()

            for i in range(FC):
                em("sync", lambda e, i=i: e.dma_start_transpose(
                    wdnT[:, i, :], wdnq_d[:, i * P:(i + 1) * P]), "p5_wld", 16)
            wld_mk = mark("p5_wld")
            nout_mk = []
            v5_mk = []
            ev_s = []
            uld_mks = []
            rnd_s = []
            rnd_v = []
            tmk_g = []

            def p5_loads(g):
                gpar = g % 2
                if g >= 2:
                    wt("gpsimd", "p5_t", tmk_g[g - 2])
                for i in range(FC):
                    em("gpsimd", lambda e, g=g, i=i, gpar=gpar: e.dma_start(
                        out=u_gr[gpar][:, i, :],
                        in_=u_h[i, :, g * GSZ:(g + 1) * GSZ]), "p5_uld", 16)
                uld_mks.append(mark("p5_uld"))

            def p5_rounds(g):
                gpar = g % 2
                for i in range(FC):
                    cpar = i % 2
                    ki = g * FC + i
                    wt("scalar", "p5_uld", uld_mks[g])
                    if ki >= 2:
                        wt("scalar", "p5_v", rnd_v[ki - 2])
                    em("scalar", lambda e, i=i, gpar=gpar, cpar=cpar: e.activation(
                        p5c[cpar][:, :], u_gr[gpar][:, i, :], AF_.Identity,
                        scale=bcol[:, B_IU:B_IU + 1]), "p5_s", 1)
                    rnd_s.append(mark("p5_s"))
                    wt("vector", "p5_s", rnd_s[-1])
                    em("vector", lambda e, i=i, gpar=gpar, cpar=cpar: e.tensor_scalar(
                        u_gr[gpar][:, i, :], p5c[cpar][:, :], 1.0, None,
                        OP.mult), "p5_v", 1)
                    rnd_v.append(mark("p5_v"))
                v5_mk.append(rnd_v[-1])

            p5_loads(0)
            p5_rounds(0)
            for g in range(NG):
                gpar = g % 2
                if g + 1 < NG:
                    p5_loads(g + 1)
                    p5_rounds(g + 1)
                wt("tensor", "p5_wld", wld_mk)
                wt("tensor", "p5_v", v5_mk[g])
                for tt in range(GSZ // P):
                    gt = g * (GSZ // P) + tt
                    dpar = gt % 2
                    if gt >= 2:
                        wt("tensor", "p5_ev", ev_s[gt - 2][0])
                    for i in range(FC):
                        for half in range(2):
                            sl = slice(half * 512, (half + 1) * 512)
                            em("tensor", lambda e, i=i, tt=tt, sl=sl, dpar=dpar,
                                   gpar=gpar: e.matmul(
                                psDN[dpar][:, sl],
                                u_gr[gpar][:, i, tt * P:(tt + 1) * P],
                                wdnT[:, i, sl],
                                start=(i == 0), stop=(i == FC - 1)),
                               "p5_t", 1 if (i == FC - 1 and half == 1) else 0)
                    tmk = mark("p5_t")
                    if tt == GSZ // P - 1:
                        tmk_g.append(tmk)
                    wt("vector", "p5_t", tmk)
                    if gt >= 1:
                        wt("vector", "p5_ev", ev_s[gt - 1][1])
                    em("vector", lambda e, gt=gt, dpar=dpar: e.scalar_tensor_tensor(
                        r2_t[:, :], psDN[dpar][:, :], bcol[:, B_SO:B_SO + 1],
                        x1b[:, gt, :], OP.mult, OP.add,
                        accum_out=rsum_c[:, gt:gt + 1]), "p5_ev", 1)
                    r2m = mark("p5_ev")
                    wt("scalar", "p5_ev", r2m)
                    em("scalar", lambda e, gt=gt: e.activation(
                        sq5[:, :], r2_t[:, :], AF_.Square,
                        accum_out=rsq_c[:, gt:gt + 1]), "p5_ev", 1)
                    sqm = mark("p5_ev")
                    wt("vector", "p5_ev", sqm)
                    fem("vector", lambda e, gt=gt: e.tensor_scalar(
                        tiny[:, 2:3], rsum_c[:, gt:gt + 1], INV1024, None,
                        OP.mult))
                    fem("vector", lambda e: e.tensor_tensor(
                        tiny[:, 3:4], tiny[:, 2:3], tiny[:, 2:3], OP.mult))
                    em("vector", lambda e, gt=gt: e.scalar_tensor_tensor(
                        tiny[:, 4:5], rsq_c[:, gt:gt + 1], INV1024, tiny[:, 3:4],
                        OP.mult, OP.subtract), "p5_ev", 1)
                    vcm = mark("p5_ev")
                    wt("scalar", "p5_ev", vcm)
                    em("scalar", lambda e: e.activation(
                        tiny[:, 5:6], tiny[:, 4:5], AF_.Sqrt, bias=np.float32(1e-5)),
                       "p5_ev", 1)
                    scm = mark("p5_ev")
                    wt("vector", "p5_ev", scm)
                    opar = gt % 2
                    if gt >= 2:
                        wt("vector", "p5_out", nout_mk[gt - 2])
                    fem("vector", lambda e: e.reciprocal(tiny[:, 6:7], tiny[:, 5:6]))
                    em("vector", lambda e, opar=opar: e.tensor_scalar(
                        out_sb[opar][:, :], r2_t[:, :], tiny[:, 2:3],
                        tiny[:, 6:7], OP.subtract, OP.mult), "p5_ev", 1)
                    fin = mark("p5_ev")
                    ev_s.append((r2m, sqm, fin))
                    wt("sync", "p5_ev", fin)
                    em("sync", lambda e, gt=gt, opar=opar: e.dma_start(
                        out=out_d[gt * P:(gt + 1) * P, :],
                        in_=out_sb[opar][:, :]), "p5_out", 16)
                    nout_mk.append(mark("p5_out"))
            wt(me, "p5_out", nout_mk[-1])
            if DBG:
                wt("sync", "s_dbg", mark("s_dbg"))

        for name in ENGINES:
            getattr(block, name)(lambda e, name=name: walk(name, e))

    return nc


_NC_CACHE = {}


def kernel(L0_attn_x, L0_attn_W_q, L0_attn_W_k, L0_attn_W_v,
           L0_ffn_w_gate, L0_ffn_w_up, L0_ffn_w_down):
    n = 8
    x = np.ascontiguousarray(np.asarray(L0_attn_x, dtype=np.float32))
    wqT = np.ascontiguousarray(np.asarray(L0_attn_W_q, dtype=np.float32).T)
    wkT = np.ascontiguousarray(np.asarray(L0_attn_W_k, dtype=np.float32).T)
    wvT = np.ascontiguousarray(np.asarray(L0_attn_W_v, dtype=np.float32).T)
    wup = np.ascontiguousarray(np.asarray(L0_ffn_w_up, dtype=np.float32))
    wg = np.ascontiguousarray(np.asarray(L0_ffn_w_gate, dtype=np.float32))
    wdn = np.ascontiguousarray(np.asarray(L0_ffn_w_down, dtype=np.float32))

    if "nc" not in _NC_CACHE:
        _NC_CACHE["nc"] = build_nc(n)
    nc = _NC_CACHE["nc"]

    in_maps = []
    for c in range(n):
        xs = np.ascontiguousarray(x[2 * c:2 * c + 2].reshape(NTOK, D))
        in_maps.append({"x": xs, "wqT": wqT, "wkT": wkT, "wvT": wvT,
                        "wup": wup, "wg": wg, "wdn": wdn})
    res = run_bass_kernel_spmd(nc, in_maps, core_ids=list(range(n)))
    outs = [np.asarray(res.results[c]["out"]).reshape(2, 2048, D) for c in range(n)]
    full = np.concatenate(outs, axis=0).astype(np.float32)
    if DBG:
        kernel.last_results = res.results
    return full


# version 10
# speedup vs baseline: 1.0231x; 1.0231x over previous
import numpy as np
import concourse.bass as bass
import concourse.mybir as mybir
from concourse.bass_utils import run_bass_kernel_spmd
from contextlib import ExitStack

F32 = mybir.dt.float32
I32 = mybir.dt.int32
BF16 = mybir.dt.bfloat16
AX = mybir.AxisListType.X
OP = mybir.AluOpType
AF_ = mybir.ActivationFunctionType

P = 128
NTOK = 4096
NT = NTOK // P
NBT = NT // 2
D = 1024
DC = D // P
FF = 3072
FC = FF // P
TC = NTOK // 512
NG = max(1, NTOK // 1024)
GSZ = NTOK // NG
EH = D // 512
INV7 = float(np.float32(1.0) / np.float32(7.0))
INVQ = float(np.float32(1.0) / np.float32(32767.0))
INV1024 = None
INV65536 = float(np.float32(1.0 / 65536.0))
EPS12 = 1e-12
SHIFT = 32768.5
SHIFTB = np.float32(32768.5)
UNSHIFT = -32768.0


def _refresh():
    global NT, NBT, DC, FC, TC, NG, GSZ, EH, INV1024
    NT = NTOK // P
    NBT = NT // 2
    DC = D // P
    FC = FF // P
    TC = NTOK // 512
    NG = max(1, NTOK // 1024)
    GSZ = NTOK // NG
    EH = D // 512
    INV1024 = float(np.float32(1.0) / np.float32(D))


_refresh()
ENGINES = ("sync", "tensor", "vector", "scalar", "gpsimd")

DBG = False

S_M1, S_S1, S_INV1, S_MG, S_SG, S_IGEFF, S_MUP, S_SUP, S_IUPEFF, \
    S_MT, S_B65, S_SG2, S_ITEFF, S_MU, S_PRE, S_SO, S_IUEFF, \
    S_T0, S_T1, S_T2, S_BG, S_X1 = range(22)
B_INV1, B_IG, B_SG, B_IUP, B_IT, B_IU, B_SO, B_BG = range(8)


class Sem:
    __slots__ = ("h", "v")

    def __init__(self, h):
        self.h = h
        self.v = 0


def build_nc(n_cores=8):
    nc = bass.Bass("TRN2", target_bir_lowering=False, detect_race_conditions=False)

    x_d = nc.dram_tensor("x", [NTOK, D], F32, kind="ExternalInput").ap()
    wqT_d = nc.dram_tensor("wqT", [D, D], F32, kind="ExternalInput").ap()
    wkT_d = nc.dram_tensor("wkT", [D, D], F32, kind="ExternalInput").ap()
    wvT_d = nc.dram_tensor("wvT", [D, D], F32, kind="ExternalInput").ap()
    wup_d = nc.dram_tensor("wup", [FF, D], F32, kind="ExternalInput").ap()
    wg_d = nc.dram_tensor("wg", [FF, D], F32, kind="ExternalInput").ap()
    wdn_d = nc.dram_tensor("wdn", [D, FF], F32, kind="ExternalInput").ap()
    out_d = nc.dram_tensor("out", [NTOK, D], F32, kind="ExternalOutput").ap()

    wupq_d = nc.dram_tensor("wupq", [FF, D], BF16).ap()
    wgq_d = nc.dram_tensor("wgq", [FF, D], BF16).ap()
    wdnq_d = nc.dram_tensor("wdnq", [D, FF], BF16).ap()
    up_h = nc.dram_tensor("up_h", [FC, P, NTOK], BF16).ap()
    gate_h = nc.dram_tensor("gate_h", [FC, P, NTOK], BF16).ap()
    t_h = nc.dram_tensor("t_h", [FC, P, NTOK], BF16).ap()
    u_h = nc.dram_tensor("u_h", [FC, P, NTOK], BF16).ap()
    sc_h = nc.dram_tensor("sc_h", [1, 8], F32).ap()
    colstage_d = nc.dram_tensor("colstage", [1, P], F32).ap()
    cc_in = [nc.dram_tensor(f"cc{i}_in", [1, 2], F32) for i in range(4)]
    cc_out = [nc.dram_tensor(f"cc{i}_out", [1, 2], F32) for i in range(4)]

    dbg = {}
    if DBG:
        dbg["x1"] = nc.dram_tensor("dbg_x1", [NT, P, D], BF16, kind="ExternalOutput").ap()
        dbg["scal"] = nc.dram_tensor("dbg_scal", [1, 24], F32, kind="ExternalOutput").ap()
        dbg["gf"] = nc.dram_tensor("dbg_gf", [2, D], F32, kind="ExternalOutput").ap()
        dbg["q1T"] = nc.dram_tensor("dbg_q1T", [P, DC, NTOK], BF16, kind="ExternalOutput").ap()
        dbg["cols"] = nc.dram_tensor("dbg_cols", [6, P, NT], F32, kind="ExternalOutput").ap()
        dbg["Qall"] = nc.dram_tensor("dbg_Qall", [P, NT // 2, D], BF16, kind="ExternalOutput").ap()
        dbg["gate0"] = nc.dram_tensor("dbg_gate0", [P, NTOK], BF16, kind="ExternalOutput").ap()
        dbg["wgq0"] = nc.dram_tensor("dbg_wgq0", [P, D], BF16, kind="ExternalOutput").ap()
        dbg["up0"] = nc.dram_tensor("dbg_up0", [P, NTOK], BF16, kind="ExternalOutput").ap()
        dbg["wupq0"] = nc.dram_tensor("dbg_wupq0", [P, D], BF16, kind="ExternalOutput").ap()
        dbg["t0"] = nc.dram_tensor("dbg_t0", [P, NTOK], BF16, kind="ExternalOutput").ap()
        dbg["u0"] = nc.dram_tensor("dbg_u0", [P, NTOK], BF16, kind="ExternalOutput").ap()
        dbg["uall"] = nc.dram_tensor("dbg_uall", [FC, P, NTOK], BF16, kind="ExternalOutput").ap()
        dbg["tall"] = nc.dram_tensor("dbg_tall", [FC, P, NTOK], BF16, kind="ExternalOutput").ap()
        dbg["upall"] = nc.dram_tensor("dbg_upall", [FC, P, NTOK], BF16, kind="ExternalOutput").ap()
        dbg["gall"] = nc.dram_tensor("dbg_gall", [FC, P, NTOK], BF16, kind="ExternalOutput").ap()
        dbg["wdnq"] = nc.dram_tensor("dbg_wdnq", [D, FF], BF16, kind="ExternalOutput").ap()
        dbg["wupq"] = nc.dram_tensor("dbg_wupq_full", [FF, D], BF16, kind="ExternalOutput").ap()

    ctx = ExitStack()
    with ctx:
        sbt = lambda name, shape, dt: ctx.enter_context(nc.sbuf_tensor(name, shape, dt))

        ident_b = sbt("ident_b", [P, P], BF16)
        ident_f = sbt("ident_f", [P, P], F32)
        ones_row = sbt("ones_row", [1, P], F32)
        c2048 = sbt("c2048", [P, 1], BF16)
        x1b = sbt("x1b", [P, NT, D], BF16)
        ssqq = sbt("ssqq", [P, NT], F32)
        ssqk = sbt("ssqk", [P, NT], F32)
        nq_c = sbt("nq_c", [P, NT], F32)
        nk_c = sbt("nk_c", [P, NT], F32)
        invq_c = sbt("invq_c", [P, NT], F32)
        invk_c = sbt("invk_c", [P, NT], F32)
        rsum_c = sbt("rsum_c", [P, NT], F32)
        rsq_c = sbt("rsq_c", [P, NT], F32)
        x1max_c = sbt("x1max_c", [P, NT], F32)
        gmax_c = sbt("gmax_c", [P, FC * TC], F32)
        upmax_c = sbt("upmax_c", [P, FC * TC], F32)
        tmax_c = sbt("tmax_c", [P, FC * TC], F32)
        umax_c = sbt("umax_c", [P, FC * 4], F32)
        tiny = sbt("tiny", [P, 8], F32)
        scal = sbt("scal", [1, 24], F32)
        rowred = sbt("rowred", [1, P], F32)
        bcol = sbt("bcol", [P, 8], F32)

        from math import prod as _prod

        class Arena:
            def __init__(self, name, nelems, dt):
                self.t = sbt(name, [P, nelems], dt)
                self.n = nelems
                self.off = 0
                self.stack = []

            def alloc(self, shape):
                assert shape[0] == P or shape[0] == 1
                n = _prod(shape[1:])
                off = (self.off + 15) // 16 * 16
                self.off = off + n
                assert self.off <= self.n, (self.t.name, self.off, self.n)
                ap = self.t.ap()[0:shape[0], off:off + n]
                if len(shape) == 3:
                    ap = ap.rearrange("p (a b) -> p a b", a=shape[1], b=shape[2])
                elif len(shape) == 4:
                    ap = ap.rearrange("p (a b c) -> p a b c", a=shape[1],
                                      b=shape[2], c=shape[3])
                return ap

            def push(self):
                self.stack.append(self.off)

            def pop(self):
                self.off = self.stack.pop()

        GSZ = 256 if NTOK >= 1024 else NTOK
        NG = NTOK // GSZ
        ARF = Arena("arena_f32", 6496, F32)
        AI = Arena("arena_i32", 4160, I32)
        AB = Arena("arena_b16", 49152, BF16)
        FMX = max(D, FF)

        CW0 = min(512, D, FF)
        ARF.push(); AI.push(); AB.push()
        wnat = [AI.alloc([P, CW0]).bitcast(F32) for _ in range(4)]
        ybuf0 = AI.alloc([P, CW0]).bitcast(F32)
        cbuf0 = AI.alloc([P, CW0])
        wdqb = [AB.alloc([P, CW0]) for _ in range(4)]
        bmax0 = ARF.alloc([P, CW0 // 32])
        sblk0 = ARF.alloc([P, CW0 // 32])
        isblk0 = ARF.alloc([P, CW0 // 32])
        wqkvT = AB.alloc([P, 3, DC, D])
        xbuf = [ARF.alloc([P, D]) for _ in range(2)]
        xTb = AB.alloc([P, DC, P])
        Qall = AB.alloc([P, NBT, D])
        _ktile1 = AB.alloc([P, D])
        _kvst1 = AB.alloc([P, D])

        class _OnePar:
            def __init__(self, ap):
                self.ap = ap

            def __getitem__(self, key):
                return self.ap[key[0], key[2]]

        ktile = _OnePar(_ktile1)
        kvst = _OnePar(_kvst1)
        sqscr = ARF.alloc([P, D])
        gf_sb = AI.alloc([1, D]).bitcast(F32)
        attn_t = ARF.alloc([P, D])
        r1_t = ARF.alloc([P, D])
        xa = ARF.alloc([P, D])
        gfb_sb = AB.alloc([P, D])
        ARF.pop(); AI.pop(); AB.pop()

        ARF.push(); AI.push(); AB.push()
        q1T = AB.alloc([P, DC, NTOK])
        wT_i = AB.alloc([P, 2, DC, P])
        mm_sb = [AB.alloc([P, NTOK]) for _ in range(2)]
        c2t = [AI.alloc([P, D]) for _ in range(2)]
        g_in = [AB.alloc([P, 512]) for _ in range(4)]
        p3cg = [AI.alloc([P, 512]) for _ in range(2)]
        p3sig = [AB.alloc([P, 512]) for _ in range(2)]
        p3csq = [AI.alloc([P, 512]) for _ in range(2)]
        t_sb = [AB.alloc([P, 512]) for _ in range(6)]
        ARF.pop(); AI.pop(); AB.pop()

        QW = 1024
        ND4 = 6
        ARF.push(); AI.push(); AB.push()
        p4t = [AB.alloc([P, QW]) for _ in range(ND4)]
        p4u = [AB.alloc([P, QW]) for _ in range(ND4)]
        p4c1 = [AI.alloc([P, QW]) for _ in range(2)]
        p4c2 = [AI.alloc([P, QW]) for _ in range(2)]
        uout = [AB.alloc([P, QW]) for _ in range(ND4)]
        ARF.pop(); AI.pop(); AB.pop()

        ARF.push(); AI.push(); AB.push()
        u_gr = [AB.alloc([P, FC, GSZ]) for _ in range(2)]
        wdnT = AB.alloc([P, FC, D])
        p5c = [AI.alloc([P, GSZ]) for _ in range(2)]
        r2_t = ARF.alloc([P, D])
        sq5 = ARF.alloc([P, D])
        out_sb = [ARF.alloc([P, D]) for _ in range(2)]
        ARF.pop(); AI.pop(); AB.pop()

        ps1 = ExitStack()
        psA = ps1.enter_context(nc.psum_tensor("psA", [P, D], F32))
        psK = ps1.enter_context(nc.psum_tensor("psK", [P, D], F32))
        psV = ps1.enter_context(nc.psum_tensor("psV", [P, D], F32))
        psGF = ps1.enter_context(nc.psum_tensor("psGF", [1, D], F32))
        ps1.close()
        ps2 = ExitStack()
        psTR = [ps2.enter_context(nc.psum_tensor(f"psTR{i}", [P, DC, P], BF16)) for i in range(2)]
        psMM = [ps2.enter_context(nc.psum_tensor(f"psMM{i}", [P, 512], F32)) for i in range(2)]
        ps2.close()
        ps5 = ExitStack()
        psDN = [ps5.enter_context(nc.psum_tensor(f"psDN{i}", [P, D], F32)) for i in range(2)]
        ps5.close()

        sem_names = ["ph", "g0", "g_wld", "p0_ld", "p0_v", "p0_sp",
                     "s_xld", "s_x2", "s_xtT", "s_xtS", "s_qkv", "s_ev",
                     "s_kvs", "s_gf", "s_gfe", "s_att", "s_v1b", "s_s1b",
                     "s_cc", "s_bc", "s_sc", "s_red",
                     "s_tr", "s_trS", "s_trV", "s_wld", "s_mm", "s_mev",
                     "s_mrv", "s_sp2",
                     "p3_ld", "p3_v", "p3_s", "p3_g", "p3_sp",
                     "p4_ld", "p4_v", "p4_s", "p4_g", "p4_sp", "p4_sp2",
                     "p5_uld", "p5_wld", "p5_v", "p5_s", "p5_g", "p5_t",
                     "p5_ev", "p5_out", "s_dbg", "fen_v", "fen_s", "fen_g"]
        sems = {n: ctx.enter_context(nc.semaphore(n)) for n in sem_names}

        block = ctx.enter_context(nc.Block())

        def walk(me, E):
            S = {k: Sem(v) for k, v in sems.items()}
            nbar = [0]

            def em(eng_name, fn, sem=None, n=1):
                inst = fn(E) if eng_name == me else None
                if sem is not None and n:
                    S[sem].v += n
                    if inst is not None:
                        inst.then_inc(S[sem].h, n)
                return S[sem].v if sem is not None else 0

            def wt(eng_name, sem, val):
                if eng_name == me and val > 0:
                    E.wait_ge(S[sem].h, val)

            def mark(sem):
                return S[sem].v

            FSEM = {"vector": "fen_v", "scalar": "fen_s", "gpsimd": "fen_g"}

            def fem(eng_name, fn, *_ignore, **_k):
                fs = FSEM[eng_name]
                em(eng_name, fn, fs, 1)
                wt(eng_name, fs, S[fs].v)

            def barrier():
                nbar[0] += 1
                em(me, lambda e: e.drain(), "ph", 1)
                S["ph"].v = 5 * nbar[0]
                wt(me, "ph", 5 * nbar[0])

            em("gpsimd", lambda e: e.memset(ident_b[:, :], 0.0), "g0", 1)
            wt("gpsimd", "g0", mark("g0"))
            em("gpsimd", lambda e: e.affine_select(
                out=ident_b[:, :], in_=ident_b[:, :], compare_op=OP.not_equal,
                fill=1.0, base=0, pattern=[[-1, P]], channel_multiplier=1),
               "g0", 1)
            em("gpsimd", lambda e: e.memset(ident_f[:, :], 0.0), "g0", 1)
            wt("gpsimd", "g0", mark("g0"))
            em("gpsimd", lambda e: e.affine_select(
                out=ident_f[:, :], in_=ident_f[:, :], compare_op=OP.not_equal,
                fill=1.0, base=0, pattern=[[-1, P]], channel_multiplier=1),
               "g0", 1)
            em("gpsimd", lambda e: e.memset(ones_row[:, :], 1.0), "g0", 1)
            em("gpsimd", lambda e: e.memset(scal[:, :], 0.0), "g0", 1)
            em("gpsimd", lambda e: e.memset(bcol[:, :], 0.0), "g0", 1)
            em("gpsimd", lambda e: e.memset(c2048[:, :], 1.0 / 2048.0), "g0", 1)
            g0_done = mark("g0")

            for wi, wsrc in enumerate([wqT_d, wkT_d, wvT_d]):
                em("gpsimd", lambda e, wi=wi, wsrc=wsrc: e.dma_start(
                    out=wqkvT[:, wi, :, :],
                    in_=wsrc.rearrange("(c p) e -> p c e", p=P)), "g_wld", 16)
            m_wqkv = mark("g_wld")

            NB0 = CW0 // 32
            p0_chunks = []
            for (w_in, wq_out, ntl, Fw) in [(wg_d, wgq_d, FC, D),
                                            (wup_d, wupq_d, FC, D),
                                            (wdn_d, wdnq_d, DC, FF)]:
                for j0 in range(ntl):
                    for cc in range(Fw // CW0):
                        p0_chunks.append((w_in, wq_out, j0, cc))
            NC0 = len(p0_chunks)
            p0_y = []
            p0_w = []
            p0_spm = []

            def p0_chunk(c):
                w_in, wq_out, j0, cc = p0_chunks[c]
                b = c % 4
                sp4 = c % 4
                csl = slice(cc * CW0, (cc + 1) * CW0)
                if c >= 4:
                    wt("gpsimd", "fen_v", p0_y[c - 4])
                em("gpsimd", lambda e, w_in=w_in, j0=j0, b=b, csl=csl: e.dma_start(
                    out=wnat[b][:, :], in_=w_in[j0 * P:(j0 + 1) * P, csl]),
                   "p0_ld", 16)
                if c == NC0 - 1:
                    for kd in range(2):
                        wt("gpsimd", "fen_v", p0_y[NC0 - 4 + kd])
                        em("gpsimd", lambda e, bd=(c + 1 + kd) % 4: e.dma_start(
                            out=wnat[bd][:, :], in_=wg_d[0:P, 0:CW0]),
                           "p0_ld", 16)
                wt("vector", "p0_ld", 16 * min(c + 3, NC0 + 2))
                fem("vector", lambda e, b=b: e.tensor_reduce(
                    bmax0[:, :], wnat[b][:, :].rearrange("p (nb k) -> p nb k", k=32),
                    AX, OP.max, apply_absolute_value=True))
                fem("vector", lambda e: e.tensor_scalar(
                    sblk0[:, :], bmax0[:, :], INV7, EPS12, OP.mult, OP.max))
                fem("vector", lambda e: e.reciprocal(isblk0[:, :], sblk0[:, :]))
                fem("vector", lambda e, b=b: e.tensor_tensor(
                    ybuf0[:, :].rearrange("p (nb k) -> p nb k", k=32),
                    wnat[b][:, :].rearrange("p (nb k) -> p nb k", k=32),
                    isblk0[:, :].unsqueeze(2).broadcast_to([P, NB0, 32]),
                    OP.mult))
                p0_y.append(mark("fen_v"))
                fem("vector", lambda e: e.tensor_copy(cbuf0[:, :], ybuf0[:, :]))
                if c >= 4:
                    wt("vector", "p0_sp", p0_spm[c - 4])
                em("vector", lambda e, sp4=sp4: e.tensor_tensor(
                    wdqb[sp4][:, :].rearrange("p (nb k) -> p nb k", k=32),
                    cbuf0[:, :].rearrange("p (nb k) -> p nb k", k=32),
                    sblk0[:, :].unsqueeze(2).broadcast_to([P, NB0, 32]),
                    OP.mult), "p0_v", 1)
                p0_w.append(mark("p0_v"))
                if c >= 2:
                    p0_spill(c - 2)

            def p0_spill(c):
                w_in, wq_out, j0, cc = p0_chunks[c]
                csl = slice(cc * CW0, (cc + 1) * CW0)
                wt("scalar", "p0_v", p0_w[c])
                em("scalar", lambda e, wq_out=wq_out, j0=j0, sp4=c % 4,
                       csl=csl: e.dma_start(
                    out=wq_out[j0 * P:(j0 + 1) * P, csl], in_=wdqb[sp4][:, :]),
                   "p0_sp", 16)
                p0_spm.append(mark("p0_sp"))

            p0_it = [0]

            def p0_drip():
                p0_it[0] += 1
                tgt = NC0 * p0_it[0] // (2 * NT)
                while len(p0_w) < min(tgt, NC0):
                    p0_chunk(len(p0_w))

            wt("tensor", "g_wld", m_wqkv)
            wt("tensor", "g0", g0_done)
            xld_mk = []
            xtS_mk = []
            ev_mk = []
            kvs_mk = []
            gf_mk = []
            x2_mk = []
            v1b_r1 = []
            s1b_sq = []
            v1b_x1 = []
            att_mk = [None, None]
            gfe_ev = [None, None]

            def p1a_tile(j, astr):
                bat, jb = j // NBT, j % NBT
                par = j % 2
                xb = xa if astr else xbuf[par]
                if astr:
                    if jb >= 1:
                        wt("sync", "s_xtS", xtS_mk[j - 1])
                elif j >= 2:
                    wt("sync", "s_xtS", xtS_mk[j - 2])
                em("sync", lambda e, j=j, xb=xb: e.dma_start(
                    out=xb[:, :], in_=x_d[j * P:(j + 1) * P, :]), "s_xld", 16)
                xld_mk.append(mark("s_xld"))
                wt("tensor", "s_xld", xld_mk[j])
                if j >= 1:
                    wt("tensor", "s_ev", ev_mk[j - 1])
                if bat == 1 and jb == 0:
                    wt("tensor", "s_gfe", gfe_ev[0])
                for d in range(DC):
                    em("tensor", lambda e, xb=xb, d=d: e.transpose(
                        psA[:, :].rearrange("p (c q) -> p c q", q=P)[:, d, :],
                        xb[:, d * P:(d + 1) * P], ident_f[:, :]),
                       "s_xtT", 1 if d == DC - 1 else 0)
                tT = mark("s_xtT")
                wt("scalar", "s_xtT", tT)
                em("scalar", lambda e: e.activation(
                    xTb[:, :, :], psA[:, :].rearrange("p (c q) -> p c q", q=P),
                    AF_.Copy), "s_xtS", 1)
                xtS_mk.append(mark("s_xtS"))
                wt("tensor", "s_xtS", xtS_mk[j])
                if j >= 1:
                    wt("tensor", "s_kvs", kvs_mk[j - 1])
                for d in range(DC):
                    for half in range(EH):
                        sl = slice(half * 512, (half + 1) * 512)
                        em("tensor", lambda e, d=d, sl=sl: e.matmul(
                            psA[:, sl], xTb[:, d, :], wqkvT[:, 0, d, sl],
                            start=(d == 0), stop=(d == DC - 1)))
                        em("tensor", lambda e, d=d, sl=sl: e.matmul(
                            psK[:, sl], xTb[:, d, :], wqkvT[:, 1, d, sl],
                            start=(d == 0), stop=(d == DC - 1)))
                        em("tensor", lambda e, d=d, sl=sl: e.matmul(
                            psV[:, sl], xTb[:, d, :], wqkvT[:, 2, d, sl],
                            start=(d == 0), stop=(d == DC - 1)),
                           "s_qkv", 1 if (d == DC - 1 and half == EH - 1) else 0)
                qkv_mk = mark("s_qkv")
                wt("scalar", "s_qkv", qkv_mk)
                if j >= 1:
                    wt("scalar", "s_kvs", kvs_mk[j - 1])
                if bat == 1:
                    wt("scalar", "s_v1b", v1b_r1[jb])
                em("scalar", lambda e, j=j: e.activation(
                    sqscr[:, :], psA[:, :], AF_.Square,
                    accum_out=ssqq[:, j:j + 1]))
                em("scalar", lambda e, jb=jb: e.activation(
                    Qall[:, jb, :], psA[:, :], AF_.Copy))
                em("scalar", lambda e, j=j: e.activation(
                    sqscr[:, :], psK[:, :], AF_.Square,
                    accum_out=ssqk[:, j:j + 1]))
                em("scalar", lambda e, par=par: e.activation(
                    ktile[:, par, :], psK[:, :], AF_.Copy), "s_ev", 1)
                wt("scalar", "s_ev", mark("s_ev"))
                em("scalar", lambda e, j=j: e.activation(
                    nq_c[:, j:j + 1], ssqq[:, j:j + 1], AF_.Sqrt))
                em("scalar", lambda e, j=j: e.activation(
                    nk_c[:, j:j + 1], ssqk[:, j:j + 1], AF_.Sqrt), "s_ev", 1)
                ev_mk.append(mark("s_ev"))
                wt("vector", "s_ev", ev_mk[j])
                fem("vector", lambda e, j=j: e.tensor_scalar(
                    tiny[:, 0:1], nq_c[:, j:j + 1], EPS12, None, OP.max))
                fem("vector", lambda e, j=j: e.reciprocal(
                    invq_c[:, j:j + 1], tiny[:, 0:1]))
                fem("vector", lambda e, j=j: e.tensor_scalar(
                    tiny[:, 1:2], nk_c[:, j:j + 1], EPS12, None, OP.max))
                fem("vector", lambda e, j=j: e.reciprocal(
                    invk_c[:, j:j + 1], tiny[:, 1:2]))
                if j >= 1:
                    wt("vector", "s_gf", gf_mk[j - 1])
                em("vector", lambda e, j=j, par=par: e.scalar_tensor_tensor(
                    kvst[:, par, :], ktile[:, par, :], invk_c[:, j:j + 1],
                    psV[:, :], OP.mult, OP.mult), "s_kvs", 1)
                kvs_mk.append(mark("s_kvs"))
                wt("tensor", "s_kvs", kvs_mk[j])
                if bat == 1 and jb == 0:
                    wt("tensor", "s_gfe", gfe_ev[0])
                for half in range(EH):
                    sl = slice(half * 512, (half + 1) * 512)
                    em("tensor", lambda e, par=par, sl=sl, jb=jb: e.matmul(
                        psGF[:, sl], c2048[:, :], kvst[:, par, sl],
                        start=(jb == 0), stop=(jb == NBT - 1)),
                       "s_gf", 1 if half == EH - 1 else 0)
                gf_mk.append(mark("s_gf"))
                p0_drip()

            def gf_block(bat):
                wt("scalar", "s_gf", gf_mk[-1])
                if bat == 1:
                    wt("scalar", "s_att", att_mk[0])
                    wt("scalar", "s_v1b", v1b_r1[NBT - 1])
                em("scalar", lambda e: e.activation(
                    gf_sb[:, :], psGF[:, :], AF_.Copy), "s_gfe", 1)
                gfe_mk = mark("s_gfe")
                if DBG:
                    wt("sync", "s_gfe", gfe_mk)
                    em("sync", lambda e, bat=bat: e.dma_start(
                        out=dbg["gf"][bat:bat + 1, :], in_=gf_sb[:, :]),
                       "s_dbg", 16)
                wt("tensor", "s_gfe", gfe_mk)
                wt("tensor", "s_ev", ev_mk[-1])
                for half in range(EH):
                    sl = slice(half * 512, (half + 1) * 512)
                    em("tensor", lambda e, sl=sl: e.matmul(
                        psA[:, sl], ones_row[:, :], gf_sb[:, sl],
                        start=True, stop=True), "s_att",
                       1 if half == EH - 1 else 0)
                att_mk[bat] = mark("s_att")
                wt("scalar", "s_att", att_mk[bat])
                em("scalar", lambda e: e.activation(
                    gfb_sb[:, :], psA[:, :], AF_.Copy), "s_gfe", 1)
                gfe_ev[bat] = mark("s_gfe")

            def p1b_tile(j):
                bat, jb = j // NBT, j % NBT
                par = j % 2
                if jb >= 2:
                    wt("sync", "s_v1b", v1b_r1[j - 2])
                elif bat == 0:
                    wt("sync", "s_xtS", xtS_mk[NBT - 2 + jb])
                else:
                    wt("sync", "s_v1b", v1b_r1[NBT - 2 + jb])
                em("sync", lambda e, j=j, par=par: e.dma_start(
                    out=xbuf[par][:, :], in_=x_d[j * P:(j + 1) * P, :]),
                   "s_x2", 16)
                x2_mk.append(mark("s_x2"))
                wt("vector", "s_gfe", gfe_ev[bat])
                wt("vector", "s_x2", x2_mk[-1])
                if jb >= 1:
                    wt("vector", "s_s1b", s1b_sq[j - 1])
                elif bat == 1:
                    wt("vector", "s_s1b", s1b_sq[-1])
                fem("vector", lambda e, jb=jb, j=j: e.scalar_tensor_tensor(
                    attn_t[:, :], Qall[:, jb, :], invq_c[:, j:j + 1],
                    gfb_sb[:, :], OP.mult, OP.mult))
                em("vector", lambda e, j=j, par=par: e.scalar_tensor_tensor(
                    r1_t[:, :], attn_t[:, :], 0.0, xbuf[par][:, :],
                    OP.add, OP.add, accum_out=rsum_c[:, j:j + 1]), "s_v1b", 1)
                v1b_r1.append(mark("s_v1b"))
                wt("scalar", "s_v1b", v1b_r1[-1])
                em("scalar", lambda e, j=j: e.activation(
                    sqscr[:, :], r1_t[:, :], AF_.Square,
                    accum_out=rsq_c[:, j:j + 1]), "s_s1b", 1)
                s1b_sq.append(mark("s_s1b"))
                wt("vector", "s_s1b", s1b_sq[-1])
                fem("vector", lambda e, j=j: e.tensor_scalar(
                    tiny[:, 2:3], rsum_c[:, j:j + 1], INV1024, None, OP.mult))
                fem("vector", lambda e: e.tensor_tensor(
                    tiny[:, 3:4], tiny[:, 2:3], tiny[:, 2:3], OP.mult))
                em("vector", lambda e, j=j: e.scalar_tensor_tensor(
                    tiny[:, 4:5], rsq_c[:, j:j + 1], INV1024, tiny[:, 3:4],
                    OP.mult, OP.subtract), "s_v1b", 1)
                vv = mark("s_v1b")
                wt("scalar", "s_v1b", vv)
                em("scalar", lambda e: e.activation(
                    tiny[:, 5:6], tiny[:, 4:5], AF_.Sqrt, bias=np.float32(1e-5)),
                   "s_s1b", 1)
                sv = mark("s_s1b")
                wt("vector", "s_s1b", sv)
                fem("vector", lambda e: e.reciprocal(tiny[:, 6:7], tiny[:, 5:6]))
                fem("vector", lambda e, j=j: e.tensor_scalar(
                    x1b[:, j, :], r1_t[:, :], tiny[:, 2:3], tiny[:, 6:7],
                    OP.subtract, OP.mult))
                em("vector", lambda e, j=j: e.tensor_reduce(
                    x1max_c[:, j:j + 1], x1b[:, j, :], AX, OP.max,
                    apply_absolute_value=True), "s_v1b", 1)
                p0_drip()

            for jb in range(NBT):
                p1a_tile(jb, False)
            gf_block(0)
            for jb in range(NBT):
                p1b_tile(jb)
                p1a_tile(NBT + jb, True)
            gf_block(1)
            for jb in range(NBT):
                p1b_tile(NBT + jb)
            while len(p0_w) < NC0:
                p0_chunk(len(p0_w))
            p0_spill(NC0 - 2)
            p0_spill(NC0 - 1)
            wt("scalar", "p0_sp", p0_spm[-1])
            barrier()

            if DBG:
                em("sync", lambda e: e.dma_start(
                    out=dbg["x1"].rearrange("n p d -> p n d"),
                    in_=x1b[:, :, :]), "s_dbg", 16)
                em("sync", lambda e: e.dma_start(
                    out=dbg["Qall"], in_=Qall[:, :, :]), "s_dbg", 16)
                for q, colt in enumerate([invq_c, invk_c, rsum_c, rsq_c,
                                          ssqq, ssqk]):
                    em("sync", lambda e, q=q, colt=colt: e.dma_start(
                        out=dbg["cols"].rearrange("k p n -> p k n")[:, q, :],
                        in_=colt[:, :]), "s_dbg", 16)

            def global_max_to(cols_list, cc_idx, scal_dst_list):
                nred = len(cols_list)
                for q, (cols, ncols) in enumerate(cols_list):
                    em("vector", lambda e, cols=cols, ncols=ncols: e.tensor_reduce(
                        tiny[:, 0:1], cols[:, 0:ncols], AX, OP.max), "s_red", 1)
                    wt("sync", "s_red", mark("s_red"))
                    em("sync", lambda e: e.dma_start(
                        out=colstage_d.rearrange("o p -> p o"),
                        in_=tiny[:, 0:1]), "s_bc", 16)
                    wt("sync", "s_bc", mark("s_bc"))
                    em("sync", lambda e: e.dma_start(
                        out=rowred[:, :], in_=colstage_d), "s_bc", 16)
                    wt("vector", "s_bc", mark("s_bc"))
                    em("vector", lambda e, q=q: e.tensor_reduce(
                        scal[:, S_T0 + q:S_T0 + q + 1], rowred[:, :], AX,
                        OP.max), "s_red", 1)
                wt("sync", "s_red", mark("s_red"))
                em("sync", lambda e, cc_idx=cc_idx, nred=nred: e.dma_start(
                    out=cc_in[cc_idx].ap()[0:1, 0:nred],
                    in_=scal[:, S_T0:S_T0 + nred]), "s_bc", 16)
                wt("gpsimd", "s_bc", mark("s_bc"))
                em("gpsimd", lambda e, cc_idx=cc_idx, nred=nred: e.collective_compute(
                    "AllReduce", OP.max, replica_groups=[list(range(n_cores))],
                    ins=[cc_in[cc_idx].ap()[0:1, 0:nred]],
                    outs=[cc_out[cc_idx].ap()[0:1, 0:nred]]), "s_cc", 1)
                wt("sync", "s_cc", mark("s_cc"))
                for q, dstn in enumerate(scal_dst_list):
                    em("sync", lambda e, cc_idx=cc_idx, q=q, dstn=dstn: e.dma_start(
                        out=scal[:, dstn:dstn + 1],
                        in_=cc_out[cc_idx].ap()[0:1, q:q + 1]), "s_bc", 16)
                wt("vector", "s_bc", mark("s_bc"))
                wt("sync", "s_bc", mark("s_bc"))

            def bcast(scal_idx, bcol_idx, hb_idx):
                wt("sync", "s_sc", mark("s_sc"))
                em("sync", lambda e, scal_idx=scal_idx, hb_idx=hb_idx: e.dma_start(
                    out=sc_h[0:1, hb_idx:hb_idx + 1],
                    in_=scal[:, scal_idx:scal_idx + 1]), "s_bc", 16)
                wt("sync", "s_bc", mark("s_bc"))
                em("sync", lambda e, bcol_idx=bcol_idx, hb_idx=hb_idx: e.dma_start(
                    out=bcol[:, bcol_idx:bcol_idx + 1],
                    in_=sc_h[0:1, hb_idx:hb_idx + 1].broadcast_to([P, 1])),
                   "s_bc", 16)
                wt("sync", "s_bc", mark("s_bc"))

            global_max_to([(x1max_c, NT)], 0, [S_M1])
            fem("vector", lambda e: e.tensor_scalar(
                scal[:, S_S1:S_S1 + 1], scal[:, S_M1:S_M1 + 1], INVQ, EPS12,
                OP.mult, OP.max))
            fem("vector", lambda e: e.reciprocal(
                scal[:, S_INV1:S_INV1 + 1], scal[:, S_S1:S_S1 + 1]))
            em("vector", lambda e: e.engine_nop(), "s_sc", 1)
            bcast(S_INV1, B_INV1, 0)
            barrier()

            trS_mk = []
            trV_mk = []
            for j in range(NT):
                par = j % 2
                if j >= 2:
                    wt("tensor", "s_trS", trS_mk[j - 2])
                for d in range(DC):
                    em("tensor", lambda e, j=j, d=d, par=par: e.transpose(
                        psTR[par][:, d, :], x1b[:, j, d * P:(d + 1) * P],
                        ident_b[:, :]), "s_tr", 1 if d == DC - 1 else 0)
                trm = mark("s_tr")
                wt("scalar", "s_tr", trm)
                if j >= 2:
                    wt("scalar", "s_trV", trV_mk[j - 2][1])
                em("scalar", lambda e, par=par: e.activation(
                    c2t[par][:, :],
                    psTR[par][:, :, :].rearrange("p c q -> p (c q)"),
                    AF_.Identity, scale=bcol[:, B_INV1:B_INV1 + 1]),
                   "s_trS", 1)
                trS_mk.append(mark("s_trS"))
                wt("vector", "s_trS", trS_mk[j])
                em("vector", lambda e, j=j, par=par: e.tensor_scalar(
                    q1T[:, :, j * P:(j + 1) * P],
                    c2t[par][:, :].rearrange("p (c q) -> p c q", q=P),
                    1.0, None, OP.mult), "s_trV", 1)
                trV_mk.append((mark("s_trV"), mark("s_trV")))
            if DBG:
                wt("sync", "s_trV", trV_mk[-1][1])
                em("sync", lambda e: e.dma_start(out=dbg["q1T"], in_=q1T[:, :, :]),
                   "s_dbg", 16)

            def mm_phase(wq_src, out_hbm, max_cols, p3=False):
                mm_grp = []
                wld_i = []
                mev_mks = []
                mrv_mks = []
                sp_mks = []
                p3_cg, p3_csq, p3_tt, p3_spm, p3_ldm = [], [], [], [], []

                def load_wi(i):
                    wpar = i % 2
                    for d in range(DC):
                        em("sync", lambda e, i=i, d=d, wpar=wpar,
                               wq_src=wq_src: e.dma_start_transpose(
                            wT_i[:, wpar, d, :],
                            wq_src[i * P:(i + 1) * P, d * P:(d + 1) * P]),
                           "s_wld", 16)
                    wld_i.append(mark("s_wld"))

                NK3 = FC * TC

                def p3_spill(k):
                    i3, tc3 = k // TC, k % TC
                    csl = slice(tc3 * 512, (tc3 + 1) * 512)
                    wt("gpsimd", "p3_v", p3_tt[k])
                    em("gpsimd", lambda e, i3=i3, par=k % 6, csl=csl: e.dma_start(
                        out=t_h[i3, :, csl], in_=t_sb[par][:, :]), "p3_sp", 16)
                    p3_spm.append(mark("p3_sp"))

                def p3_sub(k):
                    i3, tc3 = k // TC, k % TC
                    par = k % 2
                    qar = k % 4
                    sar = k % 6
                    csl = slice(tc3 * 512, (tc3 + 1) * 512)
                    if k >= 4:
                        p3_spill(k - 4)
                    if k >= 4:
                        wt("gpsimd", "fen_s", p3_cg[k - 4])
                    em("gpsimd", lambda e, i3=i3, qar=qar, csl=csl: e.dma_start(
                        out=g_in[qar][:, :], in_=gate_h[i3, :, csl]), "p3_ld", 16)
                    p3_ldm.append(mark("p3_ld"))
                    if k == NK3 - 1:
                        for kd in range(2):
                            qd = (k + 1 + kd) % 4
                            wt("gpsimd", "fen_s", p3_cg[NK3 - 4 + kd])
                            em("gpsimd", lambda e, qd=qd: e.dma_start(
                                out=g_in[qd][:, :], in_=gate_h[0, :, 0:512]),
                               "p3_ld", 16)
                    wt("scalar", "p3_ld", 16 * min(k + 3, NK3 + 2))
                    if k >= 2:
                        wt("scalar", "p3_v", p3_tt[k - 2])
                    fem("scalar", lambda e, par=par, qar=qar: e.activation(
                        p3cg[par][:, :], g_in[qar][:, :], AF_.Identity,
                        scale=bcol[:, B_IG:B_IG + 1]))
                    p3_cg.append(mark("fen_s"))
                    fem("scalar", lambda e, par=par: e.activation(
                        p3sig[par][:, :], p3cg[par][:, :], AF_.Sigmoid,
                        scale=bcol[:, B_SG:B_SG + 1]))
                    em("scalar", lambda e, par=par: e.activation(
                        p3csq[par][:, :], p3sig[par][:, :], AF_.Identity,
                        scale=np.float32(65536.0)), "p3_s", 1)
                    p3_csq.append(mark("p3_s"))
                    wt("vector", "p3_s", p3_csq[-1])
                    if k >= 6:
                        wt("vector", "p3_sp", p3_spm[k - 6])
                    em("vector", lambda e, par=par, sar=sar: e.tensor_tensor(
                        t_sb[sar][:, :], p3cg[par][:, :], p3csq[par][:, :],
                        OP.mult), "p3_v", 1)
                    p3_tt.append(mark("p3_v"))
                    wt("vector", "p3_v", p3_tt[-1])
                    em("vector", lambda e, k=k, sar=sar: e.tensor_reduce(
                        tmax_c[:, k:k + 1], t_sb[sar][:, :], AX, OP.max,
                        apply_absolute_value=True), "p3_v", 1)

                load_wi(0)
                wt("tensor", "s_trV", trV_mk[-1][1])
                for i in range(FC):
                    spar = i % 2
                    wpar = i % 2
                    if i + 1 < FC:
                        if i >= 1:
                            wt("sync", "s_mm", mm_grp[(i - 1) * TC + TC - 1])
                        load_wi(i + 1)
                    wt("tensor", "s_wld", wld_i[i])
                    for tc in range(TC):
                        gidx = i * TC + tc
                        mpar = gidx % 2
                        if gidx >= 2:
                            wt("tensor", "s_mev", mev_mks[gidx - 2])
                        for d in range(DC):
                            em("tensor", lambda e, d=d, tc=tc, mpar=mpar,
                                   wpar=wpar: e.matmul(
                                psMM[mpar][:, :],
                                wT_i[:, wpar, d, :],
                                q1T[:, d, tc * 512:(tc + 1) * 512],
                                start=(d == 0), stop=(d == DC - 1)),
                               "s_mm", 1 if d == DC - 1 else 0)
                        mm_grp.append(mark("s_mm"))
                        wt("vector", "s_mm", mm_grp[-1])
                        if tc == 0 and i >= 2:
                            wt("vector", "s_sp2", sp_mks[i - 2])
                        em("vector", lambda e, spar=spar, tc=tc, mpar=mpar:
                           e.tensor_copy(
                            mm_sb[spar][:, tc * 512:(tc + 1) * 512],
                            psMM[mpar][:, :]), "s_mev", 1)
                        mev_mks.append(mark("s_mev"))
                        wt("vector", "s_mev", mev_mks[-1])
                        em("vector", lambda e, spar=spar, tc=tc, gidx=gidx,
                               max_cols=max_cols: e.tensor_reduce(
                            max_cols[:, gidx:gidx + 1],
                            mm_sb[spar][:, tc * 512:(tc + 1) * 512], AX, OP.max,
                            apply_absolute_value=True), "s_mrv", 1)
                        if p3:
                            p3_sub(gidx)
                    mrv_mks.append(mark("s_mrv"))
                    wt("sync", "s_mev", mev_mks[-1])
                    em("sync", lambda e, i=i, spar=spar, out_hbm=out_hbm: e.dma_start(
                        out=out_hbm[i, :, :], in_=mm_sb[spar][:, :]), "s_sp2", 16)
                    sp_mks.append(mark("s_sp2"))
                wt("sync", "s_sp2", sp_mks[-1])
                wt("vector", "s_mrv", mark("s_mrv"))
                if p3:
                    for kt in range(FC * TC - 4, FC * TC):
                        p3_spill(kt)
                    wt("gpsimd", "p3_sp", p3_spm[-1])

            mm_phase(wgq_d, gate_h, gmax_c)
            barrier()

            if DBG:
                em("sync", lambda e: e.dma_start(
                    out=dbg["gate0"], in_=gate_h[0, :, :]), "s_dbg", 16)
                em("sync", lambda e: e.dma_start(
                    out=dbg["wgq0"], in_=wgq_d[0:P, :]), "s_dbg", 16)
            global_max_to([(gmax_c, FC * TC)], 1, [S_MG])
            fem("vector", lambda e: e.tensor_tensor(
                scal[:, S_T2:S_T2 + 1], scal[:, S_S1:S_S1 + 1],
                scal[:, S_MG:S_MG + 1], OP.mult))
            fem("vector", lambda e: e.tensor_scalar(
                scal[:, S_SG:S_SG + 1], scal[:, S_T2:S_T2 + 1], INVQ, EPS12,
                OP.mult, OP.max))
            fem("vector", lambda e: e.reciprocal(
                scal[:, S_T2:S_T2 + 1], scal[:, S_SG:S_SG + 1]))
            fem("vector", lambda e: e.tensor_tensor(
                scal[:, S_IGEFF:S_IGEFF + 1], scal[:, S_S1:S_S1 + 1],
                scal[:, S_T2:S_T2 + 1], OP.mult))
            em("vector", lambda e: e.engine_nop(), "s_sc", 1)
            bcast(S_IGEFF, B_IG, 1)
            bcast(S_SG, B_SG, 2)
            barrier()

            mm_phase(wupq_d, up_h, upmax_c, p3=True)
            barrier()

            if DBG:
                em("sync", lambda e: e.dma_start(
                    out=dbg["up0"], in_=up_h[0, :, :]), "s_dbg", 16)
                em("sync", lambda e: e.dma_start(
                    out=dbg["wupq0"], in_=wupq_d[0:P, :]), "s_dbg", 16)
                em("sync", lambda e: e.dma_start(
                    out=dbg["wupq"], in_=wupq_d[:, :]), "s_dbg", 16)
                em("sync", lambda e: e.dma_start(
                    out=dbg["t0"], in_=t_h[0, :, :]), "s_dbg", 16)
            global_max_to([(tmax_c, FC * TC), (upmax_c, FC * TC)], 2, [S_MT, S_MUP])
            fem("vector", lambda e: e.tensor_tensor(
                scal[:, S_T2:S_T2 + 1], scal[:, S_S1:S_S1 + 1],
                scal[:, S_MUP:S_MUP + 1], OP.mult))
            fem("vector", lambda e: e.tensor_scalar(
                scal[:, S_SUP:S_SUP + 1], scal[:, S_T2:S_T2 + 1], INVQ, EPS12,
                OP.mult, OP.max))
            fem("vector", lambda e: e.reciprocal(
                scal[:, S_T2:S_T2 + 1], scal[:, S_SUP:S_SUP + 1]))
            fem("vector", lambda e: e.tensor_tensor(
                scal[:, S_IUPEFF:S_IUPEFF + 1], scal[:, S_S1:S_S1 + 1],
                scal[:, S_T2:S_T2 + 1], OP.mult))
            fem("vector", lambda e: e.tensor_scalar(
                scal[:, S_B65:S_B65 + 1], scal[:, S_SG:S_SG + 1], INV65536,
                None, OP.mult))
            fem("vector", lambda e: e.tensor_tensor(
                scal[:, S_T2:S_T2 + 1], scal[:, S_B65:S_B65 + 1],
                scal[:, S_MT:S_MT + 1], OP.mult))
            fem("vector", lambda e: e.tensor_scalar(
                scal[:, S_SG2:S_SG2 + 1], scal[:, S_T2:S_T2 + 1], INVQ, EPS12,
                OP.mult, OP.max))
            fem("vector", lambda e: e.reciprocal(
                scal[:, S_T2:S_T2 + 1], scal[:, S_SG2:S_SG2 + 1]))
            fem("vector", lambda e: e.tensor_tensor(
                scal[:, S_ITEFF:S_ITEFF + 1], scal[:, S_B65:S_B65 + 1],
                scal[:, S_T2:S_T2 + 1], OP.mult))
            em("vector", lambda e: e.engine_nop(), "s_sc", 1)
            bcast(S_IUPEFF, B_IUP, 3)
            bcast(S_ITEFF, B_IT, 4)
            barrier()

            NPQ = NTOK // QW
            NK4 = NPQ * FC
            p4tld, p4uld, p4c1m, p4c2m, p4ttm, p4spm = [], [], [], [], [], []
            for k in range(NK4):
                i, h = k // NPQ, k % NPQ
                par = k % 2
                qar = k % ND4
                csl = slice(h * QW, (h + 1) * QW)
                if k >= ND4:
                    wt("sync", "fen_s", p4c1m[k - ND4])
                    wt("gpsimd", "p4_s", p4c2m[k - ND4])
                em("sync", lambda e, i=i, qar=qar, csl=csl: e.dma_start(
                    out=p4t[qar][:, :], in_=t_h[i, :, csl]), "p4_ld", 16)
                p4tld.append(mark("p4_ld"))
                em("gpsimd", lambda e, i=i, qar=qar, csl=csl: e.dma_start(
                    out=p4u[qar][:, :], in_=up_h[i, :, csl]), "p4_g", 16)
                p4uld.append(mark("p4_g"))
                if k == NK4 - 1:
                    for kd in range(2):
                        qd = (k + 1 + kd) % ND4
                        wt("sync", "fen_s", p4c1m[NK4 - ND4 + kd])
                        em("sync", lambda e, qd=qd: e.dma_start(
                            out=p4t[qd][:, :], in_=t_h[0, :, 0:QW]),
                           "p4_ld", 16)
                        wt("gpsimd", "p4_s", p4c2m[NK4 - ND4 + kd])
                        em("gpsimd", lambda e, qd=qd: e.dma_start(
                            out=p4u[qd][:, :], in_=up_h[0, :, 0:QW]),
                           "p4_g", 16)
                if k >= 2:
                    ks = k - 2
                    wt("scalar", "p4_v", p4ttm[ks])
                    em("scalar", lambda e, i2=ks // NPQ, qar2=ks % ND4,
                           csl2=slice((ks % NPQ) * QW, (ks % NPQ + 1) * QW):
                       e.dma_start(out=u_h[i2, :, csl2], in_=uout[qar2][:, :]),
                       "p4_sp", 16)
                    p4spm.append(("p4_sp", mark("p4_sp")))
                wt("scalar", "p4_ld", 16 * min(k + 3, NK4 + 2))
                wt("scalar", "p4_g", 16 * min(k + 3, NK4 + 2))
                if k >= 2:
                    wt("scalar", "p4_v", p4ttm[k - 2])
                em("scalar", lambda e, par=par, qar=qar: e.activation(
                    p4c1[par][:, :], p4t[qar][:, :], AF_.Identity,
                    scale=bcol[:, B_IT:B_IT + 1]), "fen_s", 1)
                p4c1m.append(mark("fen_s"))
                em("scalar", lambda e, par=par, qar=qar: e.activation(
                    p4c2[par][:, :], p4u[qar][:, :], AF_.Identity,
                    scale=bcol[:, B_IUP:B_IUP + 1]), "p4_s", 1)
                p4c2m.append(mark("p4_s"))
                wt("vector", "fen_s", p4c1m[-1])
                wt("vector", "p4_s", p4c2m[-1])
                if k >= ND4:
                    wt("vector", p4spm[k - ND4][0], p4spm[k - ND4][1])
                em("vector", lambda e, par=par, qar=qar: e.tensor_tensor(
                    uout[qar][:, :], p4c1[par][:, :], p4c2[par][:, :],
                    OP.mult), "p4_v", 1)
                p4ttm.append(mark("p4_v"))
                wt("vector", "p4_v", p4ttm[-1])
                em("vector", lambda e, k=k, qar=qar: e.tensor_reduce(
                    umax_c[:, k:k + 1], uout[qar][:, :], AX, OP.max,
                    apply_absolute_value=True), "p4_v", 1)
            for ks in range(NK4 - 2, NK4):
                wt("scalar", "p4_v", p4ttm[ks])
                em("scalar", lambda e, i2=ks // NPQ, qar2=ks % ND4,
                       csl2=slice((ks % NPQ) * QW, (ks % NPQ + 1) * QW):
                   e.dma_start(out=u_h[i2, :, csl2], in_=uout[qar2][:, :]),
                   "p4_sp", 16)
                p4spm.append(("p4_sp", mark("p4_sp")))
            wt("scalar", "p4_sp", mark("p4_sp"))
            barrier()

            global_max_to([(umax_c, NPQ * FC)], 3, [S_MU])
            fem("vector", lambda e: e.tensor_tensor(
                scal[:, S_PRE:S_PRE + 1], scal[:, S_SG2:S_SG2 + 1],
                scal[:, S_SUP:S_SUP + 1], OP.mult))
            fem("vector", lambda e: e.tensor_tensor(
                scal[:, S_T2:S_T2 + 1], scal[:, S_PRE:S_PRE + 1],
                scal[:, S_MU:S_MU + 1], OP.mult))
            fem("vector", lambda e: e.tensor_scalar(
                scal[:, S_SO:S_SO + 1], scal[:, S_T2:S_T2 + 1], INVQ, EPS12,
                OP.mult, OP.max))
            fem("vector", lambda e: e.reciprocal(
                scal[:, S_T2:S_T2 + 1], scal[:, S_SO:S_SO + 1]))
            fem("vector", lambda e: e.tensor_tensor(
                scal[:, S_IUEFF:S_IUEFF + 1], scal[:, S_PRE:S_PRE + 1],
                scal[:, S_T2:S_T2 + 1], OP.mult))
            em("vector", lambda e: e.engine_nop(), "s_sc", 1)
            bcast(S_IUEFF, B_IU, 5)
            bcast(S_SO, B_SO, 6)
            if DBG:
                wt("sync", "s_bc", mark("s_bc"))
                em("sync", lambda e: e.dma_start(out=dbg["scal"], in_=scal[:, :]),
                   "s_dbg", 16)
                for i4 in range(FC):
                    em("sync", lambda e, i4=i4: e.dma_start(
                        out=dbg["uall"][i4, :, :], in_=u_h[i4, :, :]),
                       "s_dbg", 16)
                    em("sync", lambda e, i4=i4: e.dma_start(
                        out=dbg["tall"][i4, :, :], in_=t_h[i4, :, :]),
                       "s_dbg", 16)
                    em("sync", lambda e, i4=i4: e.dma_start(
                        out=dbg["upall"][i4, :, :], in_=up_h[i4, :, :]),
                       "s_dbg", 16)
                    em("sync", lambda e, i4=i4: e.dma_start(
                        out=dbg["gall"][i4, :, :], in_=gate_h[i4, :, :]),
                       "s_dbg", 16)
                em("sync", lambda e: e.dma_start(out=dbg["u0"], in_=u_h[0, :, :]),
                   "s_dbg", 16)
                em("sync", lambda e: e.dma_start(out=dbg["wdnq"], in_=wdnq_d),
                   "s_dbg", 16)
                wt("sync", "s_dbg", mark("s_dbg"))
            barrier()

            for i in range(FC):
                em("sync", lambda e, i=i: e.dma_start_transpose(
                    wdnT[:, i, :], wdnq_d[:, i * P:(i + 1) * P]), "p5_wld", 16)
            wld_mk = mark("p5_wld")
            nout_mk = []
            v5_mk = []
            ev_s = []
            uld_mks = []
            rnd_s = []
            rnd_v = []
            tmk_g = []

            def p5_loads(g):
                gpar = g % 2
                if g >= 2:
                    wt("gpsimd", "p5_t", tmk_g[g - 2])
                for i in range(FC):
                    em("gpsimd", lambda e, g=g, i=i, gpar=gpar: e.dma_start(
                        out=u_gr[gpar][:, i, :],
                        in_=u_h[i, :, g * GSZ:(g + 1) * GSZ]), "p5_uld", 16)
                uld_mks.append(mark("p5_uld"))

            def p5_rounds(g):
                gpar = g % 2
                for i in range(FC):
                    cpar = i % 2
                    ki = g * FC + i
                    wt("scalar", "p5_uld", uld_mks[g])
                    if ki >= 2:
                        wt("scalar", "p5_v", rnd_v[ki - 2])
                    em("scalar", lambda e, i=i, gpar=gpar, cpar=cpar: e.activation(
                        p5c[cpar][:, :], u_gr[gpar][:, i, :], AF_.Identity,
                        scale=bcol[:, B_IU:B_IU + 1]), "p5_s", 1)
                    rnd_s.append(mark("p5_s"))
                    wt("vector", "p5_s", rnd_s[-1])
                    em("vector", lambda e, i=i, gpar=gpar, cpar=cpar: e.tensor_scalar(
                        u_gr[gpar][:, i, :], p5c[cpar][:, :], 1.0, None,
                        OP.mult), "p5_v", 1)
                    rnd_v.append(mark("p5_v"))
                v5_mk.append(rnd_v[-1])

            p5_loads(0)
            p5_rounds(0)
            for g in range(NG):
                gpar = g % 2
                if g + 1 < NG:
                    p5_loads(g + 1)
                    p5_rounds(g + 1)
                wt("tensor", "p5_wld", wld_mk)
                wt("tensor", "p5_v", v5_mk[g])
                for tt in range(GSZ // P):
                    gt = g * (GSZ // P) + tt
                    dpar = gt % 2
                    if gt >= 2:
                        wt("tensor", "p5_ev", ev_s[gt - 2][0])
                    for i in range(FC):
                        for half in range(2):
                            sl = slice(half * 512, (half + 1) * 512)
                            em("tensor", lambda e, i=i, tt=tt, sl=sl, dpar=dpar,
                                   gpar=gpar: e.matmul(
                                psDN[dpar][:, sl],
                                u_gr[gpar][:, i, tt * P:(tt + 1) * P],
                                wdnT[:, i, sl],
                                start=(i == 0), stop=(i == FC - 1)),
                               "p5_t", 1 if (i == FC - 1 and half == 1) else 0)
                    tmk = mark("p5_t")
                    if tt == GSZ // P - 1:
                        tmk_g.append(tmk)
                    wt("vector", "p5_t", tmk)
                    if gt >= 1:
                        wt("vector", "p5_ev", ev_s[gt - 1][1])
                    em("vector", lambda e, gt=gt, dpar=dpar: e.scalar_tensor_tensor(
                        r2_t[:, :], psDN[dpar][:, :], bcol[:, B_SO:B_SO + 1],
                        x1b[:, gt, :], OP.mult, OP.add,
                        accum_out=rsum_c[:, gt:gt + 1]), "p5_ev", 1)
                    r2m = mark("p5_ev")
                    wt("scalar", "p5_ev", r2m)
                    em("scalar", lambda e, gt=gt: e.activation(
                        sq5[:, :], r2_t[:, :], AF_.Square,
                        accum_out=rsq_c[:, gt:gt + 1]), "p5_ev", 1)
                    sqm = mark("p5_ev")
                    wt("vector", "p5_ev", sqm)
                    fem("vector", lambda e, gt=gt: e.tensor_scalar(
                        tiny[:, 2:3], rsum_c[:, gt:gt + 1], INV1024, None,
                        OP.mult))
                    fem("vector", lambda e: e.tensor_tensor(
                        tiny[:, 3:4], tiny[:, 2:3], tiny[:, 2:3], OP.mult))
                    em("vector", lambda e, gt=gt: e.scalar_tensor_tensor(
                        tiny[:, 4:5], rsq_c[:, gt:gt + 1], INV1024, tiny[:, 3:4],
                        OP.mult, OP.subtract), "p5_ev", 1)
                    vcm = mark("p5_ev")
                    wt("scalar", "p5_ev", vcm)
                    em("scalar", lambda e: e.activation(
                        tiny[:, 5:6], tiny[:, 4:5], AF_.Sqrt, bias=np.float32(1e-5)),
                       "p5_ev", 1)
                    scm = mark("p5_ev")
                    wt("vector", "p5_ev", scm)
                    opar = gt % 2
                    if gt >= 2:
                        wt("vector", "p5_out", nout_mk[gt - 2])
                    fem("vector", lambda e: e.reciprocal(tiny[:, 6:7], tiny[:, 5:6]))
                    em("vector", lambda e, opar=opar: e.tensor_scalar(
                        out_sb[opar][:, :], r2_t[:, :], tiny[:, 2:3],
                        tiny[:, 6:7], OP.subtract, OP.mult), "p5_ev", 1)
                    fin = mark("p5_ev")
                    ev_s.append((r2m, sqm, fin))
                    wt("sync", "p5_ev", fin)
                    em("sync", lambda e, gt=gt, opar=opar: e.dma_start(
                        out=out_d[gt * P:(gt + 1) * P, :],
                        in_=out_sb[opar][:, :]), "p5_out", 16)
                    nout_mk.append(mark("p5_out"))
            wt(me, "p5_out", nout_mk[-1])
            if DBG:
                wt("sync", "s_dbg", mark("s_dbg"))

        for name in ENGINES:
            getattr(block, name)(lambda e, name=name: walk(name, e))

    return nc


_NC_CACHE = {}


def kernel(L0_attn_x, L0_attn_W_q, L0_attn_W_k, L0_attn_W_v,
           L0_ffn_w_gate, L0_ffn_w_up, L0_ffn_w_down):
    n = 8
    x = np.ascontiguousarray(np.asarray(L0_attn_x, dtype=np.float32))
    wqT = np.ascontiguousarray(np.asarray(L0_attn_W_q, dtype=np.float32).T)
    wkT = np.ascontiguousarray(np.asarray(L0_attn_W_k, dtype=np.float32).T)
    wvT = np.ascontiguousarray(np.asarray(L0_attn_W_v, dtype=np.float32).T)
    wup = np.ascontiguousarray(np.asarray(L0_ffn_w_up, dtype=np.float32))
    wg = np.ascontiguousarray(np.asarray(L0_ffn_w_gate, dtype=np.float32))
    wdn = np.ascontiguousarray(np.asarray(L0_ffn_w_down, dtype=np.float32))

    if "nc" not in _NC_CACHE:
        _NC_CACHE["nc"] = build_nc(n)
    nc = _NC_CACHE["nc"]

    in_maps = []
    for c in range(n):
        xs = np.ascontiguousarray(x[2 * c:2 * c + 2].reshape(NTOK, D))
        in_maps.append({"x": xs, "wqT": wqT, "wkT": wkT, "wvT": wvT,
                        "wup": wup, "wg": wg, "wdn": wdn})
    res = run_bass_kernel_spmd(nc, in_maps, core_ids=list(range(n)))
    outs = [np.asarray(res.results[c]["out"]).reshape(2, 2048, D) for c in range(n)]
    full = np.concatenate(outs, axis=0).astype(np.float32)
    if DBG:
        kernel.last_results = res.results
    return full
